# version 03
# baseline (speedup 1.0000x reference)
import contextlib
import numpy as np
import ml_dtypes
import concourse.bass as bass
import concourse.mybir as mybir
from concourse.bass_utils import run_bass_kernel_spmd

F32 = mybir.dt.float32
BF16 = mybir.dt.bfloat16
AF = mybir.ActivationFunctionType
ALU = mybir.AluOpType
AX = mybir.AxisListType
BF = ml_dtypes.bfloat16

T = 2048
C = 2048
NH = 16
KV = 4
NE = 64
NI = 1024
SI = 512
NT = T // 128
NC = C // 128
TH = T // 2
SCALE = 1.0 / np.sqrt(128.0)
EPS = 1e-5

ALL_PHASES = ("A", "B", "C", "D", "EF")


def _host_consts():
    j = np.arange(64)
    invf = 10000.0 ** (-(2.0 * j) / 128.0)
    dfreq = np.concatenate([invf, invf])
    cq = np.cos(np.outer(dfreq, np.arange(NH)))
    ssq = np.sin(np.outer(dfreq, np.arange(NH)))
    ssq[:64] *= -1.0
    ck = np.cos(np.outer(dfreq, np.arange(KV)))
    ssk = np.sin(np.outer(dfreq, np.arange(KV)))
    ssk[:64] *= -1.0
    u = np.arange(512)[None, :]
    pp = np.arange(128)[:, None]
    md = np.concatenate([(u >= (r * 128 + pp)).astype(np.float32) for r in range(4)], axis=1)
    return {
        "rope_cq": (cq * SCALE).astype(np.float32),
        "rope_ssq": (ssq * SCALE).astype(np.float32),
        "rope_ck": ck.astype(np.float32),
        "rope_ssk": ssk.astype(np.float32),
        "id_bf": np.eye(128, dtype=BF),
        "id_f32": np.eye(128, dtype=np.float32),
        "ones_f32_row": np.ones((1, 128), dtype=np.float32),
        "ones_f32_one": np.ones((1, 1), dtype=np.float32),
        "maskdiag": md,
    }


class Ctr:
    def __init__(self, stack, nc, name):
        self.h = stack.enter_context(nc.semaphore(name))
        self.n = 0
        self.name = name


class Prog:

    ENGS = ("sync", "gpsimd", "tensor", "vector", "scalar")

    def __init__(self, blk):
        self.blk = blk
        self.s = {e: [] for e in self.ENGS}

    def wait(self, eng, ctr, val=None):
        v = ctr.n if val is None else val
        if v <= 0:
            return
        self.s[eng].append(lambda E, h=ctr.h, v=v: E.wait_ge(h, v))

    def op(self, eng, fname, *args, inc=None, inck=1, **kw):
        ret = None
        if inc is not None:
            inc.n += inck
            ret = inc.n

        def f(E, fname=fname, args=args, kw=kw, inc=inc, inck=inck):
            inst = getattr(E, fname)(*args, **kw)
            if inc is not None:
                inst.then_inc(inc.h, inck)
        self.s[eng].append(f)
        return ret

    def emit(self):
        for eng in self.ENGS:
            stream = self.s[eng]
            if stream:
                getattr(self.blk, eng)(lambda E, stream=stream: [f(E) for f in stream])
        self.s = {e: [] for e in self.ENGS}


def build(phases=ALL_PHASES, debug=(), n_experts=NE):
    nc = bass.Bass("TRN2", target_bir_lowering=False)
    CST = _host_consts()

    x_h = nc.dram_tensor("x", [T, C], F32, kind="ExternalInput").ap()
    wq_h = nc.dram_tensor("wq", [C, C], F32, kind="ExternalInput").ap()
    bq_h = nc.dram_tensor("bq", [C], F32, kind="ExternalInput").ap()
    wk_h = nc.dram_tensor("wk", [512, C], F32, kind="ExternalInput").ap()
    bk_h = nc.dram_tensor("bk", [512], F32, kind="ExternalInput").ap()
    wv_h = nc.dram_tensor("wv", [512, C], F32, kind="ExternalInput").ap()
    bv_h = nc.dram_tensor("bv", [512], F32, kind="ExternalInput").ap()
    wo_h = nc.dram_tensor("wo", [C, C], F32, kind="ExternalInput").ap()
    bo_h = nc.dram_tensor("bo", [C], F32, kind="ExternalInput").ap()
    gw_h = nc.dram_tensor("gate_w", [NE, C], F32, kind="ExternalInput").ap()
    guw_h = nc.dram_tensor("gu_w", [NE, NI, C], F32, kind="ExternalInput").ap()
    dpw_h = nc.dram_tensor("dp_w", [NE, C, 512], F32, kind="ExternalInput").ap()
    shg_h = nc.dram_tensor("sh_gate_w", [SI, C], F32, kind="ExternalInput").ap()
    shu_h = nc.dram_tensor("sh_up_w", [SI, C], F32, kind="ExternalInput").ap()
    shd_h = nc.dram_tensor("sh_down_w", [C, SI], F32, kind="ExternalInput").ap()
    shg1_h = nc.dram_tensor("sh_gate1_w", [1, C], F32, kind="ExternalInput").ap()
    out_h = nc.dram_tensor("out", [T, C], F32, kind="ExternalOutput").ap()

    attn_dram = nc.dram_tensor("attn_dram", [128, NC * T], BF16, kind="Internal").ap()
    xn_dram = nc.dram_tensor("xn_dram", [128, NC * T], BF16, kind="Internal").ap()

    K = {k: nc.inline_tensor(v, "k_" + k).ap() for k, v in CST.items()}

    dbg = {}

    def dbg_dump(p, tag, src_ap, shape, dt):
        if tag not in debug:
            return
        hnd = nc.dram_tensor("dbg_" + tag, list(shape), dt, kind="ExternalOutput").ap()
        dbg[tag] = hnd
        p.op("sync", "dma_start", out=hnd, in_=src_ap, inc=sto, inck=16)

    est = contextlib.ExitStack()
    with est:
        ldx = Ctr(est, nc, "ldx")
        ldw = Ctr(est, nc, "ldw")
        pe = Ctr(est, nc, "pe")
        dve = Ctr(est, nc, "dve")
        act = Ctr(est, nc, "act")
        sto = Ctr(est, nc, "sto")

        SB = lambda name, shape, dt=F32: est.enter_context(nc.sbuf_tensor(name, list(shape), dt))

        cq_sb = SB("cq", (128, NH))
        ssq_sb = SB("ssq", (128, NH))
        ck_sb = SB("ck", (128, KV))
        ssk_sb = SB("ssk", (128, KV))
        idb_sb = SB("idb", (128, 128), BF16)
        idf_sb = SB("idf", (128, 128))
        ones_row_sb = SB("ones_row", (1, 128))
        ones_one_sb = SB("ones_one", (1, 1))
        onecol_sb = SB("onecol", (128, 1))
        onesb_col_sb = SB("onesb_col", (128, 1), BF16)
        bq_sb = SB("s_bq", (128, NH))
        bk_sb = SB("s_bk", (128, KV))
        bo_sb = SB("s_bo", (128, NC))
        bvb_sb = SB("bvb", (128, 512))
        g1T_sb = SB("g1T", (128, NC), BF16)
        WT_sb = SB("s_WT", (64, T))
        mean_sb = SB("s_mean", (128, NT))
        negmean_sb = SB("s_negmean", (128, NT))
        rstd_sb = SB("s_rstd", (128, NT))
        sa_scope = contextlib.ExitStack()
        gwT_sb = sa_scope.enter_context(nc.sbuf_tensor("gwT", [128, NC * 64], F32))
        Sb_sb = sa_scope.enter_context(nc.sbuf_tensor("Sb", [128, NE], F32))
        xn_scope = contextlib.ExitStack()
        xnT1_sb = xn_scope.enter_context(nc.sbuf_tensor("xnT1", [128, NC * TH], BF16, side="right"))
        xnT2_sb = xn_scope.enter_context(nc.sbuf_tensor("xnT2", [128, NC * TH], BF16, side="right"))
        xnT = (xnT1_sb, xnT2_sb)

        with nc.Block() as blk:
            p = Prog(blk)

            with (nc.psum_tensor("ps_s1", [128, 512], F32) as ps_s1,
                  nc.psum_tensor("ps_s2", [128, 64], F32) as ps_s2,
                  nc.sbuf_tensor("gw_nat", [64, C], F32) as gw_nat,
                  nc.sbuf_tensor("row_a", [1, 2048], F32) as row_a,
                  nc.sbuf_tensor("scol", [64, 1], F32) as scol,
                  nc.sbuf_tensor("srow", [1, 64], F32) as srow):

                for name, dst in (("rope_cq", cq_sb), ("rope_ssq", ssq_sb),
                                  ("rope_ck", ck_sb), ("rope_ssk", ssk_sb),
                                  ("id_bf", idb_sb), ("id_f32", idf_sb),
                                  ("ones_f32_row", ones_row_sb), ("ones_f32_one", ones_one_sb)):
                    p.op("sync", "dma_start", out=dst[:], in_=K[name], inc=ldx, inck=16)
                p.op("sync", "dma_start", out=gw_nat[:], in_=gw_h, inc=ldx, inck=16)
                p.op("vector", "memset", onesb_col_sb[:], 1.0)
                p.op("vector", "memset", onecol_sb[:], 1.0)

                def row_to_cols(dst_sb, n, dst128=True):
                    for i in range(n):
                        p.op("tensor", "matmul", out=ps_s2[:, i:i + 1],
                             lhsT=row_a[0:1, i * 128:(i + 1) * 128], rhs=ones_one_sb[:],
                             start=True, stop=True, inc=(pe if i == n - 1 else None))
                    p.wait("vector", pe)
                    p.op("vector", "tensor_copy", out=dst_sb[:, 0:n], in_=ps_s2[:, 0:n], inc=dve)
                    p.wait("tensor", dve)

                p.wait("tensor", ldx)
                for ct in range(NC):
                    p.op("tensor", "transpose", out=ps_s1[:, (ct % 8) * 64:(ct % 8) * 64 + 64],
                         in_=gw_nat[0:64, ct * 128:(ct + 1) * 128], identity=idf_sb[0:64, 0:64],
                         inc=(pe if ct % 8 == 7 else None))
                    if ct % 8 == 7:
                        p.wait("vector", pe)
                        p.op("vector", "tensor_copy",
                             out=gwT_sb[:, (ct - 7) * 64:(ct + 1) * 64],
                             in_=ps_s1[:, 0:512], inc=dve)
                        p.wait("tensor", dve)
                p.wait("vector", ldx)
                p.op("vector", "reduce_sum", out=scol[:], in_=gw_nat[:], axis=AX.X, inc=dve)
                p.wait("tensor", dve)
                p.op("tensor", "transpose", out=ps_s2[0:1, 0:64], in_=scol[:],
                     identity=idf_sb[0:64, 0:64], inc=pe)
                p.wait("vector", pe)
                p.op("vector", "tensor_copy", out=srow[:], in_=ps_s2[0:1, 0:64], inc=dve)
                p.wait("tensor", dve)
                p.op("tensor", "matmul", out=ps_s2[:, 0:64], lhsT=ones_row_sb[:],
                     rhs=srow[:], start=True, stop=True, inc=pe)
                p.wait("vector", pe)
                p.op("vector", "tensor_copy", out=Sb_sb[:], in_=ps_s2[:, 0:64], inc=dve)
                p.wait("tensor", dve)

                for src, dst_sb, n in ((bq_h[None, :], bq_sb, NH), (bk_h[None, :], bk_sb, KV),
                                       (bo_h[None, :], bo_sb, NC), (shg1_h, g1T_sb, NC)):
                    p.wait("sync", pe)
                    p.op("sync", "dma_start", out=row_a[0:1, 0:n * 128], in_=src, inc=ldx, inck=16)
                    p.wait("tensor", ldx)
                    row_to_cols(dst_sb, n)
                p.wait("sync", pe)
                p.op("sync", "dma_start", out=row_a[0:1, 0:512], in_=bv_h[None, :], inc=ldx, inck=16)
                p.wait("tensor", ldx)
                p.op("tensor", "matmul", out=ps_s1[:, 0:512], lhsT=ones_row_sb[:],
                     rhs=row_a[0:1, 0:512], start=True, stop=True, inc=pe)
                p.wait("vector", pe)
                p.op("vector", "tensor_copy", out=bvb_sb[:], in_=ps_s1[:, 0:512], inc=dve)
                p.wait("tensor", dve)
                p.emit()

            if "A" in phases:
                with contextlib.ExitStack() as ph:
                    PH_PS = lambda n, s, d=F32: ph.enter_context(nc.psum_tensor(n, list(s), d))
                    PH_SB = lambda n, s, d=F32: ph.enter_context(nc.sbuf_tensor(n, list(s), d))
                    psA_t = PH_PS("psA_t", (128, 1024), BF16)
                    psA_f = PH_PS("psA_f", (128, 512))
                    psA_r = PH_PS("psA_r", (64, 128))
                    psA_w = PH_PS("psA_w", (128, 64))
                    psA_w2 = PH_PS("psA_w2", (64, 128))
                    xt0 = PH_SB("xt0", (128, C))
                    xt1 = PH_SB("xt1", (128, C))
                    sq = PH_SB("sq", (128, C))
                    xn_bf = PH_SB("xn_bf", (128, C), BF16)
                    xT = PH_SB("xT", (128, NC * 128))
                    raw_all = PH_SB("raw_all", (64, NT * 128))
                    stat = PH_SB("stat", (128, 16))
                    l64 = PH_SB("l64", (128, 64))
                    s64 = PH_SB("s64", (128, 64))
                    e8 = PH_SB("e8", (128, 8))
                    logits_all = PH_SB("logits_all", (128, NT * 64))
                    rawT_all = PH_SB("rawT_all", (128, NT * 64))
                    top8_all = PH_SB("top8_all", (128, NT * 8))
                    top8 = PH_SB("top8", (128, 8))
                    xts = (xt0, xt1)
                    m_pe_f32T = [0] * NT
                    m_pe_raw = [0] * NT
                    m_act_xn = [0] * NT
                    m_dve_stats = [0] * NT

                    for tt in range(NT):
                        xt = xts[tt % 2]
                        so = 8 * (tt % 2)
                        if tt >= 2:
                            p.wait("sync", pe, m_pe_f32T[tt - 2])
                            p.wait("sync", act, m_act_xn[tt - 2])
                        ld = p.op("sync", "dma_start", out=xt[:],
                                  in_=x_h[tt * 128:(tt + 1) * 128, :], inc=ldx, inck=16)
                        p.wait("scalar", ldx, ld)
                        if tt >= 1:
                            p.wait("scalar", dve, m_dve_stats[tt - 1])
                        a_sq = p.op("scalar", "activation", out=sq[:], in_=xt[:],
                                    func=AF.Square, inc=act)
                        p.wait("vector", ldx, ld)
                        if tt >= 2:
                            p.wait("vector", act, m_act_xn[tt - 2] - 1)
                        p.op("vector", "reduce_sum", out=stat[:, so + 0:so + 1], in_=xt[:], axis=AX.X)
                        p.wait("vector", act, a_sq)
                        p.op("vector", "reduce_sum", out=stat[:, so + 1:so + 2], in_=sq[:], axis=AX.X)
                        p.op("vector", "tensor_scalar", out=mean_sb[:, tt:tt + 1],
                             in0=stat[:, so + 0:so + 1], scalar1=1.0 / C, scalar2=None, op0=ALU.mult)
                        p.op("vector", "tensor_scalar", out=negmean_sb[:, tt:tt + 1],
                             in0=stat[:, so + 0:so + 1], scalar1=-1.0 / C, scalar2=None, op0=ALU.mult)
                        p.op("vector", "tensor_scalar", out=stat[:, so + 2:so + 3],
                             in0=stat[:, so + 1:so + 2], scalar1=1.0 / C, scalar2=EPS,
                             op0=ALU.mult, op1=ALU.add)
                        d_nv = p.op("vector", "scalar_tensor_tensor", out=stat[:, so + 3:so + 4],
                                    in0=mean_sb[:, tt:tt + 1], scalar=mean_sb[:, tt:tt + 1],
                                    in1=stat[:, so + 2:so + 3], op0=ALU.mult, op1=ALU.subtract,
                                    inc=dve)
                        m_dve_stats[tt] = d_nv
                        p.wait("scalar", dve, d_nv)
                        a_sd = p.op("scalar", "activation", out=stat[:, so + 4:so + 5],
                                    in_=stat[:, so + 3:so + 4], func=AF.Sqrt, scale=-1.0,
                                    inc=act)
                        p.wait("vector", act, a_sd)
                        p.op("vector", "reciprocal", out=rstd_sb[:, tt:tt + 1], in_=stat[:, so + 4:so + 5])
                        d_nb = p.op("vector", "tensor_scalar", out=stat[:, so + 5:so + 6],
                                    in0=rstd_sb[:, tt:tt + 1], scalar1=negmean_sb[:, tt:tt + 1],
                                    scalar2=None, op0=ALU.mult, inc=dve)
                        p.wait("scalar", dve, d_nb)
                        if tt >= 1:
                            p.wait("scalar", pe, m_pe_f32T[tt - 1])
                        a_xn = p.op("scalar", "activation", out=xn_bf[:], in_=xt[:], func=AF.Identity,
                                    bias=stat[:, so + 5:so + 6], scale=rstd_sb[:, tt:tt + 1], inc=act)
                        m_act_xn[tt] = a_xn
                        half = tt // (NT // 2)
                        trel = tt - half * (NT // 2)
                        p.wait("tensor", act, a_xn)
                        dstv = xnT[half][:].rearrange("p (ct t) -> p ct t", t=TH)
                        for b in range(2):
                            p.wait("tensor", dve)
                            for k in range(8):
                                ct = b * 8 + k
                                p.op("tensor", "transpose", out=psA_t[:, k * 128:(k + 1) * 128],
                                     in_=xn_bf[:, ct * 128:(ct + 1) * 128], identity=idb_sb[:],
                                     inc=(pe if k == 7 else None))
                            p.wait("vector", pe)
                            p.op("vector", "tensor_copy",
                                 out=dstv[:, b * 8:(b + 1) * 8, trel * 128:(trel + 1) * 128],
                                 in_=psA_t[:].rearrange("p (k t) -> p k t", t=128), inc=dve)
                        if tt >= 1:
                            p.wait("vector", pe, m_pe_raw[tt - 1])
                        for b in range(4):
                            p.wait("tensor", dve)
                            for k in range(4):
                                ct = b * 4 + k
                                p.op("tensor", "transpose", out=psA_f[:, k * 128:(k + 1) * 128],
                                     in_=xt[:, ct * 128:(ct + 1) * 128], identity=idf_sb[:],
                                     inc=(pe if k == 3 else None))
                            if b == 3:
                                m_pe_f32T[tt] = pe.n
                            p.wait("vector", pe)
                            p.op("vector", "tensor_copy", out=xT[:, b * 512:(b + 1) * 512],
                                 in_=psA_f[:], inc=dve)
                        p.wait("tensor", dve)
                        for ct in range(NC):
                            p.op("tensor", "matmul", out=psA_r[:],
                                 lhsT=gwT_sb[:, ct * 64:(ct + 1) * 64],
                                 rhs=xT[:, ct * 128:(ct + 1) * 128],
                                 start=(ct == 0), stop=(ct == NC - 1),
                                 inc=(pe if ct == NC - 1 else None))
                        m_pe_raw[tt] = pe.n
                        p.wait("vector", pe, m_pe_raw[tt])
                        p.op("vector", "tensor_copy", out=raw_all[:, tt * 128:(tt + 1) * 128],
                             in_=psA_r[:], inc=dve)
                        p.wait("tensor", dve)

                    SCL = PH_SB("scl", (128, 4 * NT))
                    el_all = PH_SB("el_all", (128, NT * 64))
                    e8_all = PH_SB("e8_all", (128, NT * 8))
                    g64 = PH_SB("g64", (128, NT * 64))
                    wtile = PH_SB("wtile", (128, 2 * 64))
                    p.wait("tensor", dve)
                    for tt in range(NT):
                        pt = p.op("tensor", "transpose", out=psA_w[:],
                                  in_=raw_all[0:64, tt * 128:(tt + 1) * 128],
                                  identity=idf_sb[0:64, 0:64], inc=pe)
                        p.wait("vector", pe, pt)
                        p.op("vector", "scalar_tensor_tensor", out=s64[:], in0=Sb_sb[:],
                             scalar=negmean_sb[:, tt:tt + 1], in1=psA_w[:],
                             op0=ALU.mult, op1=ALU.add, inc=dve)
                        p.wait("tensor", dve)
                        p.op("vector", "tensor_scalar", out=logits_all[:, tt * 64:(tt + 1) * 64],
                             in0=s64[:], scalar1=rstd_sb[:, tt:tt + 1], scalar2=None, op0=ALU.mult)
                    for tt in range(NT):
                        p.op("vector", "max", out=top8_all[:, tt * 8:(tt + 1) * 8],
                             in_=logits_all[:, tt * 64:(tt + 1) * 64])
                    t8v = top8_all[:].rearrange("p (t k) -> p t k", k=8)
                    sclv = SCL[:].rearrange("p (g t) -> p g t", t=NT)
                    p.op("vector", "tensor_scalar", out=sclv[:, 0:1, :].rearrange("p g t -> p (g t)"),
                         in0=t8v[:, :, 0:1].rearrange("p t k -> p (t k)"),
                         scalar1=-1.0, scalar2=None, op0=ALU.mult)
                    d_p2 = p.op("vector", "tensor_copy",
                                out=sclv[:, 1:2, :].rearrange("p g t -> p (g t)"),
                                in_=t8v[:, :, 7:8].rearrange("p t k -> p (t k)"), inc=dve)
                    p.wait("scalar", dve, d_p2)
                    for tt in range(NT):
                        p.op("scalar", "activation", out=e8_all[:, tt * 8:(tt + 1) * 8],
                             in_=top8_all[:, tt * 8:(tt + 1) * 8], func=AF.Exp,
                             bias=SCL[:, tt:tt + 1])
                        a_el = p.op("scalar", "activation", out=el_all[:, tt * 64:(tt + 1) * 64],
                                    in_=logits_all[:, tt * 64:(tt + 1) * 64], func=AF.Exp,
                                    bias=SCL[:, tt:tt + 1], inc=act)
                    p.wait("vector", act, act.n)
                    for tt in range(NT):
                        p.op("vector", "reduce_sum", out=SCL[:, 2 * NT + tt:2 * NT + tt + 1],
                             in_=e8_all[:, tt * 8:(tt + 1) * 8], axis=AX.X)
                    for tt in range(NT):
                        p.op("vector", "tensor_single_scalar", out=g64[:, tt * 64:(tt + 1) * 64],
                             in_=logits_all[:, tt * 64:(tt + 1) * 64],
                             scalar=SCL[:, NT + tt:NT + tt + 1], op=ALU.is_ge)
                    p.op("vector", "reciprocal", out=SCL[:, 3 * NT:4 * NT], in_=SCL[:, 2 * NT:3 * NT])
                    for tt in range(NT):
                        p.op("vector", "tensor_tensor", out=el_all[:, tt * 64:(tt + 1) * 64],
                             in0=el_all[:, tt * 64:(tt + 1) * 64],
                             in1=g64[:, tt * 64:(tt + 1) * 64], op=ALU.mult)
                    m_pe_wT = [0] * NT
                    for tt in range(NT):
                        wt_buf = wtile[:, (tt % 2) * 64:(tt % 2) * 64 + 64]
                        if tt >= 2:
                            p.wait("vector", pe, m_pe_wT[tt - 2])
                        d_w = p.op("vector", "tensor_scalar", out=wt_buf,
                                   in0=el_all[:, tt * 64:(tt + 1) * 64],
                                   scalar1=SCL[:, 3 * NT + tt:3 * NT + tt + 1], scalar2=None,
                                   op0=ALU.mult, inc=dve)
                        p.wait("tensor", dve, d_w)
                        m_pe_wT[tt] = p.op("tensor", "transpose", out=psA_w2[:], in_=wt_buf,
                                           identity=idf_sb[:], inc=pe)
                        p.wait("vector", pe, m_pe_wT[tt])
                        p.op("vector", "tensor_copy", out=WT_sb[:, tt * 128:(tt + 1) * 128],
                             in_=psA_w2[:], inc=dve)
                        p.wait("tensor", dve)

                    if debug:
                        p.wait("sync", dve)
                    dbg_dump(p, "xnT1", xnT1_sb[:], (128, NC * TH), BF16)
                    dbg_dump(p, "xnT2", xnT2_sb[:], (128, NC * TH), BF16)
                    dbg_dump(p, "WT", WT_sb[:], (64, T), F32)
                    dbg_dump(p, "raw", raw_all[:], (64, NT * 128), F32)
                    dbg_dump(p, "logits", logits_all[:], (128, NT * 64), F32)
                    dbg_dump(p, "rawT", rawT_all[:], (128, NT * 64), F32)
                    dbg_dump(p, "top8", top8_all[:], (128, NT * 8), F32)
                    dbg_dump(p, "Sb", Sb_sb[:], (128, NE), F32)
                    dbg_dump(p, "mean", mean_sb[:], (128, NT), F32)
                    dbg_dump(p, "rstd", rstd_sb[:], (128, NT), F32)
                    p.emit()

            sa_scope.close()

            bcd = contextlib.ExitStack()
            if "B" in phases:
                BS = lambda n, s, d=F32: bcd.enter_context(nc.sbuf_tensor(n, list(s), d))
                kT_sb = BS("kT", (128, KV * T), BF16)
                v_sb = BS("v", (128, NT * 512), BF16)
                with contextlib.ExitStack() as ph:
                    psB_t = ph.enter_context(nc.psum_tensor("psB_t", [128, 1024], BF16))
                    psB_k0 = ph.enter_context(nc.psum_tensor("psB_k0", [128, 512], F32))
                    psB_k1 = ph.enter_context(nc.psum_tensor("psB_k1", [128, 512], F32))
                    wn_sb = ph.enter_context(nc.sbuf_tensor("wn", [128, 4 * C], BF16))
                    wkT_sb = ph.enter_context(nc.sbuf_tensor("wkT", [128, NC * 512], BF16))
                    wvT_sb = ph.enter_context(nc.sbuf_tensor("wvT", [128, NC * 512], BF16))
                    kf_sb = ph.enter_context(nc.sbuf_tensor("kf", [128, 2 * 512], F32))
                    tmp_sb = ph.enter_context(nc.sbuf_tensor("tmpB", [128, 512], F32))
                    psk = (psB_k0, psB_k1)

                    def load_transpose_w(src_2d, dst_sb, nrow_tiles):
                        nrt = nrow_tiles
                        p.wait("gpsimd", pe)
                        for gi in range(nrt):
                            ldv = p.op("gpsimd", "dma_start",
                                       out=wn_sb[:, gi * C:(gi + 1) * C],
                                       in_=src_2d[gi * 128:(gi + 1) * 128, :],
                                       inc=ldw, inck=16)
                        p.wait("tensor", ldw, ldv)
                        dstv = dst_sb[:].rearrange("p (ct r) -> p ct r", r=nrt * 128)
                        for gi in range(nrt):
                            for cb in range(2):
                                p.wait("tensor", dve)
                                for k in range(8):
                                    ct = cb * 8 + k
                                    p.op("tensor", "transpose",
                                         out=psB_t[:, k * 128:(k + 1) * 128],
                                         in_=wn_sb[:, gi * C + ct * 128:gi * C + (ct + 1) * 128],
                                         identity=idb_sb[:], inc=(pe if k == 7 else None))
                                p.wait("vector", pe)
                                p.op("vector", "tensor_copy",
                                     out=dstv[:, cb * 8:(cb + 1) * 8, gi * 128:(gi + 1) * 128],
                                     in_=psB_t[:].rearrange("p (k t) -> p k t", t=128), inc=dve)

                    load_transpose_w(wk_h, wkT_sb, 4)
                    m_wkT = dve.n
                    load_transpose_w(wv_h, wvT_sb, 4)
                    m_wvT = dve.n

                    p.wait("tensor", dve, m_wkT)
                    m_rope = {}
                    for gi in range(KV):
                        for ncg in range(4):
                            t0 = ncg * 512
                            gidx = gi * 4 + ncg
                            if gidx >= 2:
                                p.wait("tensor", act, m_rope.get(("a", gidx - 2), 0))
                            for ct in range(NC):
                                p.op("tensor", "matmul", out=psk[gidx % 2][:],
                                     lhsT=wkT_sb[:, ct * 512 + gi * 128:ct * 512 + (gi + 1) * 128],
                                     rhs=xnT[t0 // TH][:, ct * TH + (t0 % TH):ct * TH + (t0 % TH) + 512],
                                     start=(ct == 0), stop=(ct == NC - 1),
                                     inc=(pe if ct == NC - 1 else None))
                            mk = pe.n
                            p.wait("scalar", pe, mk)
                            if gidx >= 2:
                                p.wait("scalar", dve, m_rope.get(("d", gidx - 2), 0))
                            kf = kf_sb[:, (gidx % 2) * 512:(gidx % 2) * 512 + 512]
                            m_rope[("a", gidx)] = p.op("scalar", "activation", out=kf,
                                                       in_=psk[gidx % 2][:], func=AF.Identity,
                                                       bias=bk_sb[:, gi:gi + 1], inc=act)
                            p.wait("vector", act, m_rope[("a", gidx)])
                            p.op("vector", "tensor_copy", out=tmp_sb[0:64, :], in_=kf.tensor[64:128, (gidx % 2) * 512:(gidx % 2) * 512 + 512])
                            p.op("vector", "tensor_copy", out=tmp_sb[64:128, :], in_=kf.tensor[0:64, (gidx % 2) * 512:(gidx % 2) * 512 + 512])
                            p.op("vector", "tensor_scalar", out=kf, in0=kf,
                                 scalar1=ck_sb[:, gi:gi + 1], scalar2=None, op0=ALU.mult)
                            m_rope[("d", gidx)] = p.op("vector", "scalar_tensor_tensor",
                                                       out=kT_sb[:, gi * T + t0:gi * T + t0 + 512],
                                                       in0=tmp_sb[:], scalar=ssk_sb[:, gi:gi + 1],
                                                       in1=kf, op0=ALU.mult, op1=ALU.add, inc=dve)

                    p.wait("tensor", dve, m_wvT)
                    p.wait("tensor", act, m_rope[("a", 15)])
                    m_vev = {}
                    for tt in range(NT):
                        if tt >= 2:
                            p.wait("tensor", dve, m_vev.get(tt - 2, 0))
                        half = tt // (NT // 2)
                        trel = tt - half * (NT // 2)
                        for ct in range(NC):
                            p.op("tensor", "matmul", out=psk[tt % 2][:],
                                 lhsT=xnT[half][:, ct * TH + trel * 128:ct * TH + (trel + 1) * 128],
                                 rhs=wvT_sb[:, ct * 512:(ct + 1) * 512],
                                 start=(ct == 0), stop=(ct == NC - 1),
                                 inc=(pe if ct == NC - 1 else None))
                            mv = pe.n
                        p.wait("vector", pe, mv)
                        m_vev[tt] = p.op("vector", "tensor_tensor", out=v_sb[:, tt * 512:(tt + 1) * 512],
                                         in0=psk[tt % 2][:], in1=bvb_sb[:], op=ALU.add, inc=dve)

                    if debug:
                        p.wait("sync", dve)
                    dbg_dump(p, "kT", kT_sb[:], (128, KV * T), BF16)
                    dbg_dump(p, "v", v_sb[:], (128, NT * 512), BF16)
                    p.emit()

            if "C" in phases:
                qT_all = bcd.enter_context(nc.sbuf_tensor("qT_all", [128, NH * T], BF16))
                with contextlib.ExitStack() as ph:
                    PH_PS = lambda n, s, d=F32: ph.enter_context(nc.psum_tensor(n, list(s), d))
                    PH_SB = lambda n, s, d=F32: ph.enter_context(nc.sbuf_tensor(n, list(s), d))
                    psC_t = PH_PS("psC_t", (128, 1024), BF16)
                    psC_s0 = PH_PS("psC_s0", (128, 512))
                    psC_s1 = PH_PS("psC_s1", (128, 512))
                    wqn_sb = PH_SB("wqn", (128, 2 * C), BF16)
                    wqT_sb = PH_SB("wqT", (128, 2 * C), BF16)
                    qf_sb = PH_SB("qf", (128, 2 * 512))
                    qtmp_sb = PH_SB("qtmp", (128, 512))
                    pss = (psC_s0, psC_s1)
                    m_wq_pe = [0] * NH
                    m_q_pe = [0] * NH
                    m_aq = 0
                    for h in range(NH):
                        par = h % 2
                        if h >= 2:
                            p.wait("gpsimd", pe, m_wq_pe[h - 2])
                        lq = p.op("gpsimd", "dma_start", out=wqn_sb[:, par * C:(par + 1) * C],
                                  in_=wq_h[h * 128:(h + 1) * 128, :], inc=ldw, inck=16)
                        p.wait("tensor", ldw, lq)
                        if h >= 2:
                            p.wait("tensor", pe, m_q_pe[h - 2])
                        wqT = wqT_sb[:, par * C:(par + 1) * C]
                        for cb in range(2):
                            p.wait("tensor", dve)
                            for k in range(8):
                                ct = cb * 8 + k
                                p.op("tensor", "transpose", out=psC_t[:, k * 128:(k + 1) * 128],
                                     in_=wqn_sb[:, par * C + ct * 128:par * C + (ct + 1) * 128],
                                     identity=idb_sb[:], inc=(pe if k == 7 else None))
                            p.wait("vector", pe)
                            p.op("vector", "tensor_copy",
                                 out=wqT.rearrange("p (cb2 r) -> p cb2 r", r=128)[:, cb * 8:(cb + 1) * 8, :],
                                 in_=psC_t[:].rearrange("p (k t) -> p k t", t=128), inc=dve)
                        m_wq_pe[h] = pe.n
                        p.wait("tensor", dve)
                        for ncg in range(4):
                            qpar = ncg % 2
                            t0 = ncg * 512
                            if h + ncg > 0:
                                p.wait("tensor", dve)
                            for ct in range(NC):
                                p.op("tensor", "matmul", out=pss[qpar][:],
                                     lhsT=wqT[:, ct * 128:(ct + 1) * 128],
                                     rhs=xnT[t0 // TH][:, ct * TH + (t0 % TH):ct * TH + (t0 % TH) + 512],
                                     start=(ct == 0), stop=(ct == NC - 1),
                                     inc=(pe if ct == NC - 1 else None))
                            mq = pe.n
                            p.wait("scalar", pe, mq)
                            qf = qf_sb[:, qpar * 512:(qpar + 1) * 512]
                            m_aq = p.op("scalar", "activation", out=qf, in_=pss[qpar][:],
                                        func=AF.Identity, bias=bq_sb[:, h:h + 1], inc=act)
                            p.wait("vector", act, m_aq)
                            p.op("vector", "tensor_copy", out=qtmp_sb[0:64, :],
                                 in_=qf_sb[64:128, qpar * 512:(qpar + 1) * 512])
                            p.op("vector", "tensor_copy", out=qtmp_sb[64:128, :],
                                 in_=qf_sb[0:64, qpar * 512:(qpar + 1) * 512])
                            p.op("vector", "tensor_scalar", out=qf, in0=qf,
                                 scalar1=cq_sb[:, h:h + 1], scalar2=None, op0=ALU.mult)
                            p.op("vector", "scalar_tensor_tensor",
                                 out=qT_all[:, h * T + t0:h * T + t0 + 512],
                                 in0=qtmp_sb[:], scalar=ssq_sb[:, h:h + 1],
                                 in1=qf, op0=ALU.mult, op1=ALU.add, inc=dve)
                        m_q_pe[h] = pe.n
                p.wait("sync", dve)
                sp1 = p.op("sync", "dma_start", out=xn_dram[:, 0:NC * TH], in_=xnT1_sb[:],
                           inc=sto, inck=16)
                sp2 = p.op("sync", "dma_start", out=xn_dram[:, NC * TH:], in_=xnT2_sb[:],
                           inc=sto, inck=16)
                p.emit()
                xn_scope.close()

                ctxT_sb = bcd.enter_context(nc.sbuf_tensor("ctxT", [128, NH * T], BF16))
                with contextlib.ExitStack() as ph:
                    PH_PS = lambda n, s, d=F32: ph.enter_context(nc.psum_tensor(n, list(s), d))
                    PH_SB = lambda n, s, d=F32: ph.enter_context(nc.sbuf_tensor(n, list(s), d))
                    psC_s0 = PH_PS("psC2_s0", (128, 512))
                    psC_s1 = PH_PS("psC2_s1", (128, 512))
                    psC_o = PH_PS("psC_o", (128, 512))
                    psC_d = PH_PS("psC_d", (1, 512))
                    psC_b = PH_PS("psC_b", (128, 512))
                    maskd_sb = PH_SB("maskd", (128, 4 * 512))
                    exp_sb = PH_SB("expS", (128, NT * 512), BF16)
                    den_sb = PH_SB("den", (1, 2 * 512))
                    rden_sb = PH_SB("rden", (1, 2 * 512))
                    rdb_sb = PH_SB("rdb", (128, 512))
                    pss = (psC_s0, psC_s1)
                    lm = p.op("sync", "dma_start", out=maskd_sb[:], in_=K["maskdiag"],
                              inc=ldx, inck=16)
                    p.wait("vector", ldx, lm)
                    p.wait("vector", sto, sp2)
                    p.wait("tensor", dve)
                    exp_hist = []
                    m_av = {}
                    m_ctx = 0
                    m_ad = 0
                    m_dr = 0
                    for h in range(NH):
                        g = h // 4
                        for tc in range(4):
                            for st in range(NT):
                                spar = len(exp_hist) % 2
                                if len(exp_hist) >= 2:
                                    p.wait("tensor", act, exp_hist[-2])
                                elif h == 0 and tc == 0 and st == 0:
                                    p.wait("tensor", act, m_aq)
                                ms = p.op("tensor", "matmul", out=pss[spar][:],
                                          lhsT=kT_sb[:, g * T + st * 128:g * T + (st + 1) * 128],
                                          rhs=qT_all[:, h * T + tc * 512:h * T + (tc + 1) * 512],
                                          start=True, stop=True, inc=pe)
                                r = st - 4 * tc
                                if 0 <= r <= 3:
                                    p.wait("vector", pe, ms)
                                    dm = p.op("vector", "tensor_tensor", out=pss[spar][:],
                                              in0=pss[spar][:], in1=maskd_sb[:, r * 512:(r + 1) * 512],
                                              op=ALU.add, inc=dve)
                                    p.wait("scalar", dve, dm)
                                else:
                                    p.wait("scalar", pe, ms)
                                if (h, tc) != (0, 0) and st == 0:
                                    p.wait("scalar", pe, m_av[(h, tc - 1) if tc > 0 else (h - 1, 3)])
                                bias_ap = onecol_sb[:] if st < 4 * tc else 0.0
                                e_ms = p.op("scalar", "activation",
                                            out=exp_sb[:, st * 512:(st + 1) * 512],
                                            in_=pss[spar][:], func=AF.Exp,
                                            bias=bias_ap, inc=act)
                                exp_hist.append(e_ms)
                            p.wait("tensor", act, exp_hist[-1])
                            if (h, tc) != (0, 0):
                                p.wait("tensor", act, m_ad)
                                p.wait("tensor", dve, m_ctx)
                            for st in range(NT):
                                p.op("tensor", "matmul", out=psC_d[:],
                                     lhsT=onesb_col_sb[:], rhs=exp_sb[:, st * 512:(st + 1) * 512],
                                     start=(st == 0), stop=(st == NT - 1),
                                     inc=(pe if st == NT - 1 else None))
                            m_denv = pe.n
                            for st in range(NT):
                                p.op("tensor", "matmul", out=psC_o[:],
                                     lhsT=v_sb[:, st * 512 + g * 128:st * 512 + (g + 1) * 128],
                                     rhs=exp_sb[:, st * 512:(st + 1) * 512],
                                     start=(st == 0), stop=(st == NT - 1),
                                     inc=(pe if st == NT - 1 else None))
                            m_av[(h, tc)] = pe.n
                            dpar = (4 * h + tc) % 2
                            p.wait("scalar", pe, m_denv)
                            p.wait("scalar", dve, m_dr)
                            m_ad = p.op("scalar", "copy", out=den_sb[0:1, dpar * 512:(dpar + 1) * 512],
                                        in_=psC_d[:], inc=act)
                            p.wait("vector", act, m_ad)
                            m_dr = p.op("vector", "reciprocal",
                                        out=rden_sb[0:1, dpar * 512:(dpar + 1) * 512],
                                        in_=den_sb[0:1, dpar * 512:(dpar + 1) * 512], inc=dve)
                            p.wait("tensor", dve, m_dr)
                            pb = p.op("tensor", "matmul", out=psC_b[:], lhsT=ones_row_sb[:],
                                      rhs=rden_sb[0:1, dpar * 512:(dpar + 1) * 512],
                                      start=True, stop=True, inc=pe)
                            p.wait("vector", pe, pb)
                            p.op("vector", "tensor_copy", out=rdb_sb[:], in_=psC_b[:], inc=dve)
                            p.wait("vector", pe, m_av[(h, tc)])
                            m_ctx = p.op("vector", "tensor_tensor",
                                         out=ctxT_sb[:, h * T + tc * 512:h * T + (tc + 1) * 512],
                                         in0=psC_o[:], in1=rdb_sb[:], op=ALU.mult, inc=dve)
                    if debug:
                        p.wait("sync", dve)
                    dbg_dump(p, "ctxT", ctxT_sb[:], (128, NH * T), BF16)
                    p.emit()

            if "D" in phases:
                with contextlib.ExitStack() as ph:
                    psD_t = ph.enter_context(nc.psum_tensor("psD_t", [128, 1024], BF16))
                    psD_a = ph.enter_context(nc.psum_tensor("psD_a", [128, 512], F32))
                    psD_b = ph.enter_context(nc.psum_tensor("psD_b", [128, 512], F32))
                    won_sb = ph.enter_context(nc.sbuf_tensor("won", [128, 2 * C], BF16))
                    woT_sb = ph.enter_context(nc.sbuf_tensor("woT", [128, 2 * C], BF16))
                    ast_sb = ph.enter_context(nc.sbuf_tensor("ast", [128, 2 * T], BF16))
                    psd = (psD_a, psD_b)
                    p.wait("gpsimd", pe)
                    p.wait("gpsimd", dve)
                    m_wo_pe = [0] * NC
                    m_out_pe = [0] * NC
                    m_ast = [0] * NC
                    m_aev = {}
                    for jo in range(NC):
                        par = jo % 2
                        if jo >= 2:
                            p.wait("gpsimd", pe, m_wo_pe[jo - 2])
                        lw = p.op("gpsimd", "dma_start", out=won_sb[:, par * C:(par + 1) * C],
                                  in_=wo_h[jo * 128:(jo + 1) * 128, :], inc=ldw, inck=16)
                        p.wait("tensor", ldw, lw)
                        if jo >= 2:
                            p.wait("tensor", pe, m_out_pe[jo - 2])
                        woT = woT_sb[:, par * C:(par + 1) * C]
                        for cb in range(2):
                            p.wait("tensor", dve)
                            for k in range(8):
                                jt = cb * 8 + k
                                p.op("tensor", "transpose", out=psD_t[:, k * 128:(k + 1) * 128],
                                     in_=won_sb[:, par * C + jt * 128:par * C + (jt + 1) * 128],
                                     identity=idb_sb[:], inc=(pe if k == 7 else None))
                            p.wait("vector", pe)
                            p.op("vector", "tensor_copy",
                                 out=woT.rearrange("p (cb2 r) -> p cb2 r", r=128)[:, cb * 8:(cb + 1) * 8, :],
                                 in_=psD_t[:].rearrange("p (k t) -> p k t", t=128), inc=dve)
                        m_wo_pe[jo] = pe.n
                        p.wait("tensor", dve)
                        if jo >= 2:
                            p.wait("scalar", sto, m_ast[jo - 2])
                        for tc in range(4):
                            tpar = tc % 2
                            if jo + tc > 0:
                                p.wait("tensor", act, m_aev.get((jo, tc - 2),
                                                               m_aev.get((jo - 1, 2 + tc), 0)))
                            for jt in range(NC):
                                p.op("tensor", "matmul", out=psd[tpar][:],
                                     lhsT=woT[:, jt * 128:(jt + 1) * 128],
                                     rhs=ctxT_sb[:, jt * T + tc * 512:jt * T + (tc + 1) * 512],
                                     start=(jt == 0), stop=(jt == NC - 1),
                                     inc=(pe if jt == NC - 1 else None))
                            mo = pe.n
                            p.wait("scalar", pe, mo)
                            m_aev[(jo, tc)] = p.op("scalar", "activation",
                                                   out=ast_sb[:, par * T + tc * 512:par * T + (tc + 1) * 512],
                                                   in_=psd[tpar][:], func=AF.Identity,
                                                   bias=bo_sb[:, jo:jo + 1], inc=act)
                        m_out_pe[jo] = pe.n
                        p.wait("sync", act, m_aev[(jo, 3)])
                        m_ast[jo] = p.op("sync", "dma_start",
                                         out=attn_dram[:, jo * T:(jo + 1) * T],
                                         in_=ast_sb[:, par * T:(par + 1) * T], inc=sto, inck=16)
                    p.wait("sync", sto)
                    dbg_dump(p, "attn", attn_dram, (128, NC * T), BF16)
                    p.emit()

            if "EF" in phases:
                bcd.close()
                guwT_dramA = nc.dram_tensor("guwT_dramA", [32 * 8 * 128, C], BF16,
                                            kind="Internal").ap()
                guwT_dramB = nc.dram_tensor("guwT_dramB", [33 * 8 * 128, C], BF16,
                                            kind="Internal").ap()
                dpwT_dram = nc.dram_tensor("dpwT_dram", [(NE + 1) * 128, 4 * C], BF16,
                                           kind="Internal").ap()

                def guwT_dram_slice(eidx, si):
                    if eidx < 32:
                        r = (eidx * 8 + si) * 128
                        return guwT_dramA[r:r + 128, :]
                    r = ((eidx - 32) * 8 + si) * 128
                    return guwT_dramB[r:r + 128, :]
                ef = contextlib.ExitStack()
                EFS = lambda n, s, d=F32: ef.enter_context(nc.sbuf_tensor(n, list(s), d))
                xnh_sb = EFS("xnh", (128, NC * TH), BF16)
                acc_sb = EFS("acc", (128, NC * TH))
                p.wait("gpsimd", pe)
                p.wait("gpsimd", dve)
                p.wait("gpsimd", act)
                p.wait("sync", pe)
                p.wait("sync", dve)
                for hf in range(2):
                    with contextlib.ExitStack() as eph:
                        PS2 = lambda n, s, d=F32: eph.enter_context(nc.psum_tensor(n + str(hf), list(s), d))
                        SB2 = lambda n, s, d=F32: eph.enter_context(nc.sbuf_tensor(n + str(hf), list(s), d))
                        ps_g = PS2("ps_g", (128, 1024))
                        ps_u = PS2("ps_u", (128, 1024))
                        ps_d0 = PS2("ps_d0", (128, 512))
                        ps_d1 = PS2("ps_d1", (128, 512))
                        ps_t0 = PS2("ps_t0", (128, 1024), BF16)
                        ps_t1 = PS2("ps_t1", (128, 1024), BF16)
                        gu_nat = SB2("gu_nat", (128, 2 * C), BF16)
                        guwT = SB2("guwT", (128, 2 * C), BF16)
                        dp_nat = SB2("dp_nat", (128, NC * 512), BF16)
                        dpwT = SB2("dpwT", (128, 4 * C), BF16)
                        interT = SB2("interT", (128, 4 * TH), BF16)
                        sg_sb = SB2("sg", (128, 1024))
                        p1_sb = sg_sb
                        Wb_sb = SB2("Wb", (128, TH))
                        wrow_sb = SB2("wrow", (1, TH))
                        shrow_sb = wrow_sb
                        psd = (ps_d0, ps_d1)
                        pst = (ps_t0, ps_t1)

                        if hf == 1:
                            p.wait("gpsimd", dve)
                            p.wait("gpsimd", pe)
                            p.wait("sync", pe)
                            p.wait("sync", dve)
                        rs = p.op("sync", "dma_start", out=xnh_sb[:],
                                  in_=xn_dram[:, hf * NC * TH:(hf + 1) * NC * TH],
                                  inc=ldx, inck=16)
                        for jo in range(NC):
                            p.op("gpsimd", "dma_start",
                                 out=acc_sb[:, jo * TH:(jo + 1) * TH],
                                 in_=attn_dram[:, jo * T + hf * TH:jo * T + hf * TH + TH],
                                 inc=ldw, inck=16)
                        m_acc_init = ldw.n
                        m_xnh = rs

                        st_dve = {"wb": 0, "intT": [0] * 4, "accl": 0}
                        st_pe = {"dp_tr": 0, "up": [0] * 8, "dn": 0,
                                 "wb": 0, "gu_done": [0] * 8}
                        tr_hist = []
                        spill_gu = [0] * 8
                        spill_dp = [0]
                        st_misc = {"dp_ldw": 0}
                        acc_hist = []
                        order = (0, 4, 1, 5, 2, 6, 3, 7)

                        def tr_batch(srcs, dst_ap):
                            b = pst[len(tr_hist) % 2]
                            if len(tr_hist) >= 2:
                                p.wait("tensor", dve, tr_hist[-2])
                            for k, s_ap in enumerate(srcs):
                                p.op("tensor", "transpose", out=b[:, k * 128:(k + 1) * 128],
                                     in_=s_ap, identity=idb_sb[:],
                                     inc=(pe if k == 7 else None))
                            p.wait("vector", pe)
                            ev = p.op("vector", "tensor_copy", out=dst_ap,
                                      in_=b[:].rearrange("p (k t) -> p k t", t=128), inc=dve)
                            tr_hist.append(ev)
                            return ev

                        def up_pair_down(eidx, acquire_gu, acquire_dp, post_up=None,
                                         next_wb=None, gu_evac_ms=None):
                            acquire_dp()
                            for si, s in enumerate(order):
                                par = si % 2
                                acquire_gu(si)
                                ev = gu_evac_ms(si) if gu_evac_ms else (0, 0)
                                dst = ps_g if si % 2 == 0 else ps_u
                                j = s % 4
                                if si % 2 == 0:
                                    p.wait("tensor", act, act.n)
                                else:
                                    p.wait("tensor", dve, st_dve["intT"][j - 1] if j > 0 else st_dve["accl"])
                                for half2 in range(2):
                                    if ev[half2]:
                                        p.wait("tensor", dve, ev[half2])
                                    for tc in range(2):
                                        for ct in range(half2 * 8, half2 * 8 + 8):
                                            p.op("tensor", "matmul",
                                                 out=dst[:, tc * 512:(tc + 1) * 512],
                                                 lhsT=guwT[:, par * C + ct * 128:par * C + (ct + 1) * 128],
                                                 rhs=xnh_sb[:, ct * TH + tc * 512:ct * TH + (tc + 1) * 512],
                                                 start=(ct == 0), stop=(ct == NC - 1),
                                                 inc=(pe if ct == NC - 1 and tc == 1 else None))
                                st_pe["up"][si] = pe.n
                                if post_up is not None:
                                    post_up(si)
                                if si % 2 == 1:
                                    p.wait("scalar", pe, st_pe["up"][si - 1])
                                    a_sg = p.op("scalar", "activation", out=sg_sb[:],
                                                in_=ps_g[:], func=AF.Silu, inc=act)
                                    p.wait("vector", act, a_sg)
                                    p.wait("vector", pe, st_pe["up"][si])
                                    p.op("vector", "tensor_tensor", out=p1_sb[:],
                                         in0=sg_sb[:], in1=ps_u[:], op=ALU.mult)
                                    st_dve["intT"][j] = p.op(
                                        "vector", "tensor_tensor",
                                        out=interT[:, j * TH:(j + 1) * TH],
                                        in0=p1_sb[:], in1=Wb_sb[:],
                                        op=ALU.mult, inc=dve)
                            if next_wb is not None:
                                next_wb()
                            p.wait("tensor", dve, st_dve["intT"][3])
                            p.wait("tensor", ldw, st_misc["dp_ldw"])
                            for nacc in range(NC * 2):
                                ht, tc = nacc // 2, nacc % 2
                                dpar = nacc % 2
                                if nacc == 0:
                                    p.wait("tensor", dve, st_dve["wb"])
                                if len(acc_hist) >= 2:
                                    p.wait("tensor", dve, acc_hist[-2])
                                for j in range(4):
                                    p.op("tensor", "matmul", out=psd[dpar][:],
                                         lhsT=dpwT[:, j * C + ht * 128:j * C + (ht + 1) * 128],
                                         rhs=interT[:, j * TH + tc * 512:j * TH + (tc + 1) * 512],
                                         start=(j == 0), stop=(j == 3),
                                         inc=(pe if j == 3 else None))
                                mdn = pe.n
                                p.wait("vector", pe, mdn)
                                av = p.op("vector", "tensor_tensor",
                                          out=acc_sb[:, ht * TH + tc * 512:ht * TH + (tc + 1) * 512],
                                          in0=psd[dpar][:],
                                          in1=acc_sb[:, ht * TH + tc * 512:ht * TH + (tc + 1) * 512],
                                          op=ALU.add, inc=dve)
                                acc_hist.append(av)
                            st_pe["dn"] = pe.n
                            st_dve["accl"] = acc_hist[-1]

                        def expert_h0(gu_srcs, dp_src, eidx, next_wb):
                            lgu = [0] * 8
                            ev_ms = [[0, 0] for _ in range(8)]

                            def gu_load(si):
                                p.wait("gpsimd", pe,
                                       st_pe["gu_done"][si - 2 if si >= 2 else si + 6])
                                lgu[si] = p.op("gpsimd", "dma_start",
                                               out=gu_nat[:, (si % 2) * C:(si % 2 + 1) * C],
                                               in_=gu_srcs[si], inc=ldw, inck=16)
                            gu_load(0)
                            gu_load(1)

                            def acquire_dp():
                                p.wait("gpsimd", pe, st_pe["dp_tr"])
                                for ht in range(NC):
                                    ldp = p.op("gpsimd", "dma_start",
                                               out=dp_nat[:, ht * 512:(ht + 1) * 512],
                                               in_=dp_src[ht * 128:(ht + 1) * 128, :],
                                               inc=ldw, inck=16)
                                p.wait("tensor", ldw, ldp)
                                dpv = dpwT[:].rearrange("p (j h) -> p j h", j=4)
                                for hb in range(8):
                                    j, hh = hb % 4, (hb // 4) * 8
                                    if hb == 0:
                                        p.wait("vector", sto, spill_dp[0])
                                    tr_batch([dp_nat[:, (hh + k) * 512 + j * 128:(hh + k) * 512 + (j + 1) * 128]
                                              for k in range(8)],
                                             dpv[:, j, hh * 128:(hh + 8) * 128])
                                st_pe["dp_tr"] = pe.n
                                p.wait("sync", dve, tr_hist[-1])
                                spill_dp[0] = p.op("sync", "dma_start",
                                                   out=dpwT_dram[eidx * 128:(eidx + 1) * 128, :],
                                                   in_=dpwT[:], inc=sto, inck=16)

                            def acquire_gu(si):
                                par = si % 2
                                p.wait("tensor", ldw, lgu[si])
                                guv = guwT[:].rearrange("p (q ct r) -> p q ct r", q=2, r=128)
                                for cb in range(2):
                                    if cb == 0:
                                        p.wait("vector", sto, spill_gu[si])
                                    tr_batch([gu_nat[:, par * C + (cb * 8 + k) * 128:par * C + (cb * 8 + k + 1) * 128]
                                              for k in range(8)],
                                             guv[:, par, cb * 8:(cb + 1) * 8, :])
                                    ev_ms[si][cb] = tr_hist[-1]
                                st_pe["gu_done"][si] = pe.n
                                if si + 2 < 8:
                                    gu_load(si + 2)
                                p.wait("sync", dve, tr_hist[-1])
                                sgu = p.op("sync", "dma_start",
                                           out=guwT_dram_slice(eidx, si),
                                           in_=guwT[:, par * C:(par + 1) * C], inc=sto, inck=16)
                                spill_gu[(si + 2) % 8] = sgu

                            up_pair_down(eidx, acquire_gu, acquire_dp,
                                         next_wb=next_wb,
                                         gu_evac_ms=lambda si: ev_ms[si])

                        def expert_h1(eidx, next_wb):
                            lgu = [0] * 8

                            def gu_load(si):
                                p.wait("gpsimd", pe,
                                       st_pe["up"][si - 2 if si >= 2 else si + 6])
                                lgu[si] = p.op("gpsimd", "dma_start",
                                               out=guwT[:, (si % 2) * C:(si % 2 + 1) * C],
                                               in_=guwT_dram_slice(eidx, si),
                                               inc=ldw, inck=16)
                            gu_load(0)
                            gu_load(1)

                            def acquire_dp():
                                p.wait("gpsimd", pe, st_pe["dn"])
                                st_misc["dp_ldw"] = p.op(
                                    "gpsimd", "dma_start", out=dpwT[:],
                                    in_=dpwT_dram[eidx * 128:(eidx + 1) * 128, :],
                                    inc=ldw, inck=16)

                            def acquire_gu(si):
                                p.wait("tensor", ldw, lgu[si])

                            def post_up(si):
                                if si + 2 < 8:
                                    gu_load(si + 2)

                            up_pair_down(eidx, acquire_gu, acquire_dp, post_up,
                                         next_wb=next_wb)

                        def wb_expert(e):
                            def f():
                                p.wait("sync", pe, st_pe["wb"])
                                lr = p.op("sync", "dma_start", out=wrow_sb[:],
                                          in_=WT_sb[e:e + 1, hf * TH:(hf + 1) * TH],
                                          inc=ldx, inck=16)
                                p.wait("tensor", ldx, lr)
                                if acc_hist:
                                    p.wait("tensor", dve, acc_hist[-1])
                                for tc in range(2):
                                    p.wait("tensor", dve, st_dve["wb"])
                                    mb = p.op("tensor", "matmul", out=psd[0][:],
                                              lhsT=ones_row_sb[:],
                                              rhs=wrow_sb[0:1, tc * 512:(tc + 1) * 512],
                                              start=True, stop=True, inc=pe)
                                    p.wait("vector", pe, mb)
                                    st_dve["wb"] = p.op("vector", "tensor_copy",
                                                        out=Wb_sb[:, tc * 512:(tc + 1) * 512],
                                                        in_=psd[0][:], inc=dve)
                                st_pe["wb"] = pe.n
                            return f

                        def wb_shared():
                            p.wait("tensor", ldx, m_xnh)
                            if acc_hist:
                                p.wait("tensor", dve, acc_hist[-1])
                            for tc in range(2):
                                p.wait("tensor", act, act.n)
                                p.wait("tensor", dve, st_dve["wb"])
                                for ct in range(NC):
                                    p.op("tensor", "matmul", out=psd[0][0:1, :],
                                         lhsT=g1T_sb[:, ct:ct + 1],
                                         rhs=xnh_sb[:, ct * TH + tc * 512:ct * TH + (tc + 1) * 512],
                                         start=(ct == 0), stop=(ct == NC - 1),
                                         inc=(pe if ct == NC - 1 else None))
                                mg = pe.n
                                p.wait("scalar", pe, mg)
                                ag = p.op("scalar", "activation",
                                          out=shrow_sb[0:1, tc * 512:(tc + 1) * 512],
                                          in_=psd[0][0:1, :], func=AF.Sigmoid, inc=act)
                                p.wait("tensor", act, ag)
                                mb = p.op("tensor", "matmul", out=psd[0][:],
                                          lhsT=ones_row_sb[:],
                                          rhs=shrow_sb[0:1, tc * 512:(tc + 1) * 512],
                                          start=True, stop=True, inc=pe)
                                p.wait("vector", pe, mb)
                                st_dve["wb"] = p.op("vector", "tensor_copy",
                                                    out=Wb_sb[:, tc * 512:(tc + 1) * 512],
                                                    in_=psd[0][:], inc=dve)
                            st_pe["wb"] = pe.n

                        p.wait("tensor", ldw, m_acc_init)
                        p.wait("tensor", ldx, m_xnh)
                        if hf == 0:
                            sh_gu = [shg_h[j * 128:(j + 1) * 128, :] if si % 2 == 0
                                     else shu_h[j * 128:(j + 1) * 128, :]
                                     for si, j in enumerate([s % 4 for s in order])]
                            wb_shared()
                            expert_h0(sh_gu, shd_h, NE,
                                      next_wb=(wb_expert(0) if n_experts else None))
                            for e in range(n_experts):
                                gu_srcs = [guw_h[e, s * 128:(s + 1) * 128, :] for s in order]
                                expert_h0(gu_srcs, dpw_h[e], e,
                                          next_wb=(wb_expert(e + 1) if e + 1 < n_experts else None))
                        else:
                            wb_shared()
                            expert_h1(NE, next_wb=(wb_expert(0) if n_experts else None))
                            for e in range(n_experts):
                                expert_h1(e, next_wb=(wb_expert(e + 1) if e + 1 < n_experts else None))

                        with contextlib.ExitStack() as fin:
                            xt_f = fin.enter_context(nc.sbuf_tensor("xt_f" + str(hf), [128, C], F32))
                            out_f = fin.enter_context(nc.sbuf_tensor("out_f" + str(hf), [128, C], F32))
                            psF = ps_g
                            m_fin_st = [0] * 8
                            m_fin_dve = [0] * 8
                            for tb in range(8):
                                if tb >= 1:
                                    p.wait("sync", dve, m_fin_dve[tb - 1])
                                lx = p.op("sync", "dma_start", out=xt_f[:],
                                          in_=x_h[hf * TH + tb * 128:hf * TH + (tb + 1) * 128, :],
                                          inc=ldx, inck=16)
                                for hb in range(2):
                                    if tb == 0 and hb == 0:
                                        p.wait("tensor", dve, acc_hist[-1])
                                        p.wait("tensor", act, act.n)
                                    else:
                                        p.wait("tensor", dve)
                                    for k in range(8):
                                        ht = hb * 8 + k
                                        p.op("tensor", "transpose",
                                             out=psF[:, k * 128:(k + 1) * 128],
                                             in_=acc_sb[:, ht * TH + tb * 128:ht * TH + (tb + 1) * 128],
                                             identity=idf_sb[:], inc=(pe if k == 7 else None))
                                    p.wait("vector", pe)
                                    p.wait("vector", ldx, lx)
                                    if tb >= 1 and hb == 0:
                                        p.wait("vector", sto, m_fin_st[tb - 1])
                                    m_fin_dve[tb] = p.op(
                                        "vector", "tensor_tensor",
                                        out=out_f[:, hb * 1024:(hb + 1) * 1024],
                                        in0=psF[:], in1=xt_f[:, hb * 1024:(hb + 1) * 1024],
                                        op=ALU.add, inc=dve)
                                p.wait("sync", dve, m_fin_dve[tb])
                                m_fin_st[tb] = p.op("sync", "dma_start",
                                                    out=out_h[hf * TH + tb * 128:hf * TH + (tb + 1) * 128, :],
                                                    in_=out_f[:],
                                                    inc=sto, inck=16)
                            p.emit()
                ef.close()
            else:
                bcd.close()

            p.wait("sync", sto)
            p.wait("sync", ldx)
            p.emit()

    return nc, dbg


def kernel(**inputs):
    nc, _ = build()
    wmap = {
        "wq": inputs["L0_attn_wq"], "bq": inputs["L0_attn_bq"],
        "wk": inputs["L0_attn_wk"], "bk": inputs["L0_attn_bk"],
        "wv": inputs["L0_attn_wv"], "bv": inputs["L0_attn_bv"],
        "wo": inputs["L0_attn_wo"], "bo": inputs["L0_attn_bo"],
        "gate_w": inputs["L0_ffn_gate_w"], "gu_w": inputs["L0_ffn_gu_w"],
        "dp_w": inputs["L0_ffn_dp_w"],
        "sh_gate_w": inputs["L0_ffn_sh_gate_w"], "sh_up_w": inputs["L0_ffn_sh_up_w"],
        "sh_down_w": inputs["L0_ffn_sh_down_w"], "sh_gate1_w": inputs["L0_ffn_sh_gate1_w"],
    }
    wmap = {k: np.ascontiguousarray(np.asarray(v, dtype=np.float32)) for k, v in wmap.items()}
    xfull = np.asarray(inputs["L0_attn_x"], dtype=np.float32)
    in_maps = [dict(wmap, x=np.ascontiguousarray(xfull[b])) for b in range(8)]
    res = run_bass_kernel_spmd(nc, in_maps, core_ids=list(range(8)))
    return np.stack([r["out"] for r in res.results], axis=0)


# version 07
# speedup vs baseline: 1.0570x; 1.0570x over previous
import contextlib
import numpy as np
import ml_dtypes
import concourse.bass as bass
import concourse.mybir as mybir
from concourse.bass_utils import run_bass_kernel_spmd

F32 = mybir.dt.float32
BF16 = mybir.dt.bfloat16
AF = mybir.ActivationFunctionType
ALU = mybir.AluOpType
AX = mybir.AxisListType
BF = ml_dtypes.bfloat16

T = 2048
C = 2048
NH = 16
KV = 4
NE = 64
NI = 1024
SI = 512
NT = T // 128
NC = C // 128
TH = T // 2
SCALE = 1.0 / np.sqrt(128.0)
EPS = 1e-5

ALL_PHASES = ("A", "B", "C", "D", "EF")


def _host_consts():
    j = np.arange(64)
    invf = 10000.0 ** (-(2.0 * j) / 128.0)
    dfreq = np.concatenate([invf, invf])
    cq = np.cos(np.outer(dfreq, np.arange(NH)))
    ssq = np.sin(np.outer(dfreq, np.arange(NH)))
    ssq[:64] *= -1.0
    ck = np.cos(np.outer(dfreq, np.arange(KV)))
    ssk = np.sin(np.outer(dfreq, np.arange(KV)))
    ssk[:64] *= -1.0
    u = np.arange(512)[None, :]
    pp = np.arange(128)[:, None]
    md = np.concatenate([(u >= (r * 128 + pp)).astype(np.float32) for r in range(4)], axis=1)
    return {
        "rope_cq": (cq * SCALE).astype(np.float32),
        "rope_ssq": (ssq * SCALE).astype(np.float32),
        "rope_ck": ck.astype(np.float32),
        "rope_ssk": ssk.astype(np.float32),
        "id_bf": np.eye(128, dtype=BF),
        "id_f32": np.eye(128, dtype=np.float32),
        "ones_f32_row": np.ones((1, 128), dtype=np.float32),
        "ones_f32_one": np.ones((1, 1), dtype=np.float32),
        "maskdiag": md,
    }


class Ctr:
    def __init__(self, stack, nc, name):
        self.h = stack.enter_context(nc.semaphore(name))
        self.n = 0
        self.name = name


class Prog:

    ENGS = ("sync", "gpsimd", "tensor", "vector", "scalar")

    def __init__(self, blk):
        self.blk = blk
        self.s = {e: [] for e in self.ENGS}

    def wait(self, eng, ctr, val=None):
        v = ctr.n if val is None else val
        if v <= 0:
            return
        self.s[eng].append(lambda E, h=ctr.h, v=v: E.wait_ge(h, v))

    def op(self, eng, fname, *args, inc=None, inck=1, **kw):
        ret = None
        if inc is not None:
            inc.n += inck
            ret = inc.n

        def f(E, fname=fname, args=args, kw=kw, inc=inc, inck=inck):
            inst = getattr(E, fname)(*args, **kw)
            if inc is not None:
                inst.then_inc(inc.h, inck)
        self.s[eng].append(f)
        return ret

    def emit(self):
        for eng in self.ENGS:
            stream = self.s[eng]
            if stream:
                getattr(self.blk, eng)(lambda E, stream=stream: [f(E) for f in stream])
        self.s = {e: [] for e in self.ENGS}


def build(phases=ALL_PHASES, debug=(), n_experts=NE):
    nc = bass.Bass("TRN2", target_bir_lowering=False)
    CST = _host_consts()

    x_h = nc.dram_tensor("x", [T, C], F32, kind="ExternalInput").ap()
    wq_h = nc.dram_tensor("wq", [C, C], F32, kind="ExternalInput").ap()
    bq_h = nc.dram_tensor("bq", [C], F32, kind="ExternalInput").ap()
    wk_h = nc.dram_tensor("wk", [512, C], F32, kind="ExternalInput").ap()
    bk_h = nc.dram_tensor("bk", [512], F32, kind="ExternalInput").ap()
    wv_h = nc.dram_tensor("wv", [512, C], F32, kind="ExternalInput").ap()
    bv_h = nc.dram_tensor("bv", [512], F32, kind="ExternalInput").ap()
    wo_h = nc.dram_tensor("wo", [C, C], F32, kind="ExternalInput").ap()
    bo_h = nc.dram_tensor("bo", [C], F32, kind="ExternalInput").ap()
    gw_h = nc.dram_tensor("gate_w", [NE, C], F32, kind="ExternalInput").ap()
    guw_h = nc.dram_tensor("gu_w", [NE, NI, C], F32, kind="ExternalInput").ap()
    dpw_h = nc.dram_tensor("dp_w", [NE, C, 512], F32, kind="ExternalInput").ap()
    shg_h = nc.dram_tensor("sh_gate_w", [SI, C], F32, kind="ExternalInput").ap()
    shu_h = nc.dram_tensor("sh_up_w", [SI, C], F32, kind="ExternalInput").ap()
    shd_h = nc.dram_tensor("sh_down_w", [C, SI], F32, kind="ExternalInput").ap()
    shg1_h = nc.dram_tensor("sh_gate1_w", [1, C], F32, kind="ExternalInput").ap()
    out_h = nc.dram_tensor("out", [T, C], F32, kind="ExternalOutput").ap()

    attn_dram = nc.dram_tensor("attn_dram", [128, NC * T], BF16, kind="Internal").ap()
    xn_dram = nc.dram_tensor("xn_dram", [128, NC * T], BF16, kind="Internal").ap()

    K = {k: nc.inline_tensor(v, "k_" + k).ap() for k, v in CST.items()}

    dbg = {}

    def dbg_dump(p, tag, src_ap, shape, dt):
        if tag not in debug:
            return
        hnd = nc.dram_tensor("dbg_" + tag, list(shape), dt, kind="ExternalOutput").ap()
        dbg[tag] = hnd
        p.op("sync", "dma_start", out=hnd, in_=src_ap, inc=sto, inck=16)

    est = contextlib.ExitStack()
    with est:
        ldx = Ctr(est, nc, "ldx")
        ldw = Ctr(est, nc, "ldw")
        pe = Ctr(est, nc, "pe")
        dve = Ctr(est, nc, "dve")
        act = Ctr(est, nc, "act")
        sto = Ctr(est, nc, "sto")

        SB = lambda name, shape, dt=F32: est.enter_context(nc.sbuf_tensor(name, list(shape), dt))

        cq_sb = SB("cq", (128, NH))
        ssq_sb = SB("ssq", (128, NH))
        ck_sb = SB("ck", (128, KV))
        ssk_sb = SB("ssk", (128, KV))
        idb_sb = SB("idb", (128, 128), BF16)
        idf_sb = SB("idf", (128, 128))
        ones_row_sb = SB("ones_row", (1, 128))
        ones_one_sb = SB("ones_one", (1, 1))
        onecol_sb = SB("onecol", (128, 1))
        onesb_col_sb = SB("onesb_col", (128, 1), BF16)
        bq_sb = SB("s_bq", (128, NH))
        bk_sb = SB("s_bk", (128, KV))
        bo_sb = SB("s_bo", (128, NC))
        bvb_sb = SB("bvb", (128, 512))
        g1T_sb = SB("g1T", (128, NC), BF16)
        WT_sb = SB("s_WT", (64, T))
        mean_sb = SB("s_mean", (128, NT))
        negmean_sb = SB("s_negmean", (128, NT))
        rstd_sb = SB("s_rstd", (128, NT))
        sa_scope = contextlib.ExitStack()
        gwT_sb = sa_scope.enter_context(nc.sbuf_tensor("gwT", [128, NC * 64], F32))
        Sb_sb = sa_scope.enter_context(nc.sbuf_tensor("Sb", [128, NE], F32))
        xn_scope = contextlib.ExitStack()
        xnT1_sb = xn_scope.enter_context(nc.sbuf_tensor("xnT1", [128, NC * TH], BF16, side="right"))
        xnT2_sb = xn_scope.enter_context(nc.sbuf_tensor("xnT2", [128, NC * TH], BF16, side="right"))
        xnT = (xnT1_sb, xnT2_sb)

        with nc.Block() as blk:
            p = Prog(blk)

            with (nc.psum_tensor("ps_s1", [128, 512], F32) as ps_s1,
                  nc.psum_tensor("ps_s2", [128, 64], F32) as ps_s2,
                  nc.sbuf_tensor("gw_nat", [64, C], F32) as gw_nat,
                  nc.sbuf_tensor("row_a", [1, 2048], F32) as row_a,
                  nc.sbuf_tensor("scol", [64, 1], F32) as scol,
                  nc.sbuf_tensor("srow", [1, 64], F32) as srow):

                for name, dst in (("rope_cq", cq_sb), ("rope_ssq", ssq_sb),
                                  ("rope_ck", ck_sb), ("rope_ssk", ssk_sb),
                                  ("id_bf", idb_sb), ("id_f32", idf_sb),
                                  ("ones_f32_row", ones_row_sb), ("ones_f32_one", ones_one_sb)):
                    p.op("sync", "dma_start", out=dst[:], in_=K[name], inc=ldx, inck=16)
                p.op("sync", "dma_start", out=gw_nat[:], in_=gw_h, inc=ldx, inck=16)
                p.op("vector", "memset", onesb_col_sb[:], 1.0)
                p.op("vector", "memset", onecol_sb[:], 1.0)

                def row_to_cols(dst_sb, n, dst128=True):
                    for i in range(n):
                        p.op("tensor", "matmul", out=ps_s2[:, i:i + 1],
                             lhsT=row_a[0:1, i * 128:(i + 1) * 128], rhs=ones_one_sb[:],
                             start=True, stop=True, inc=(pe if i == n - 1 else None))
                    p.wait("vector", pe)
                    p.op("vector", "tensor_copy", out=dst_sb[:, 0:n], in_=ps_s2[:, 0:n], inc=dve)
                    p.wait("tensor", dve)

                p.wait("tensor", ldx)
                for ct in range(NC):
                    p.op("tensor", "transpose", out=ps_s1[:, (ct % 8) * 64:(ct % 8) * 64 + 64],
                         in_=gw_nat[0:64, ct * 128:(ct + 1) * 128], identity=idf_sb[0:64, 0:64],
                         inc=(pe if ct % 8 == 7 else None))
                    if ct % 8 == 7:
                        p.wait("vector", pe)
                        p.op("vector", "tensor_copy",
                             out=gwT_sb[:, (ct - 7) * 64:(ct + 1) * 64],
                             in_=ps_s1[:, 0:512], inc=dve)
                        p.wait("tensor", dve)
                p.wait("vector", ldx)
                p.op("vector", "reduce_sum", out=scol[:], in_=gw_nat[:], axis=AX.X, inc=dve)
                p.wait("tensor", dve)
                p.op("tensor", "transpose", out=ps_s2[0:1, 0:64], in_=scol[:],
                     identity=idf_sb[0:64, 0:64], inc=pe)
                p.wait("vector", pe)
                p.op("vector", "tensor_copy", out=srow[:], in_=ps_s2[0:1, 0:64], inc=dve)
                p.wait("tensor", dve)
                p.op("tensor", "matmul", out=ps_s2[:, 0:64], lhsT=ones_row_sb[:],
                     rhs=srow[:], start=True, stop=True, inc=pe)
                p.wait("vector", pe)
                p.op("vector", "tensor_copy", out=Sb_sb[:], in_=ps_s2[:, 0:64], inc=dve)
                p.wait("tensor", dve)

                for src, dst_sb, n in ((bq_h[None, :], bq_sb, NH), (bk_h[None, :], bk_sb, KV),
                                       (bo_h[None, :], bo_sb, NC), (shg1_h, g1T_sb, NC)):
                    p.wait("sync", pe)
                    p.op("sync", "dma_start", out=row_a[0:1, 0:n * 128], in_=src, inc=ldx, inck=16)
                    p.wait("tensor", ldx)
                    row_to_cols(dst_sb, n)
                p.wait("sync", pe)
                p.op("sync", "dma_start", out=row_a[0:1, 0:512], in_=bv_h[None, :], inc=ldx, inck=16)
                p.wait("tensor", ldx)
                p.op("tensor", "matmul", out=ps_s1[:, 0:512], lhsT=ones_row_sb[:],
                     rhs=row_a[0:1, 0:512], start=True, stop=True, inc=pe)
                p.wait("vector", pe)
                p.op("vector", "tensor_copy", out=bvb_sb[:], in_=ps_s1[:, 0:512], inc=dve)
                p.wait("tensor", dve)
                p.emit()

            if "A" in phases:
                with contextlib.ExitStack() as ph:
                    PH_PS = lambda n, s, d=F32: ph.enter_context(nc.psum_tensor(n, list(s), d))
                    PH_SB = lambda n, s, d=F32: ph.enter_context(nc.sbuf_tensor(n, list(s), d))
                    psA_t = PH_PS("psA_t", (128, 1024), BF16)
                    psA_f = PH_PS("psA_f", (128, 512))
                    psA_r = PH_PS("psA_r", (64, 128))
                    psA_w = PH_PS("psA_w", (128, 64))
                    psA_w2 = PH_PS("psA_w2", (64, 128))
                    xt0 = PH_SB("xt0", (128, C))
                    xt1 = PH_SB("xt1", (128, C))
                    sq = PH_SB("sq", (128, C))
                    xn_bf = PH_SB("xn_bf", (128, C), BF16)
                    xT = PH_SB("xT", (128, NC * 128))
                    raw_all = PH_SB("raw_all", (64, NT * 128))
                    stat = PH_SB("stat", (128, 16))
                    l64 = PH_SB("l64", (128, 64))
                    s64 = PH_SB("s64", (128, 64))
                    e8 = PH_SB("e8", (128, 8))
                    logits_all = PH_SB("logits_all", (128, NT * 64))
                    rawT_all = PH_SB("rawT_all", (128, NT * 64))
                    top8_all = PH_SB("top8_all", (128, NT * 8))
                    top8 = PH_SB("top8", (128, 8))
                    xts = (xt0, xt1)
                    m_pe_f32T = [0] * NT
                    m_pe_raw = [0] * NT
                    m_act_xn = [0] * NT
                    m_dve_stats = [0] * NT

                    for tt in range(NT):
                        xt = xts[tt % 2]
                        so = 8 * (tt % 2)
                        if tt >= 2:
                            p.wait("sync", pe, m_pe_f32T[tt - 2])
                            p.wait("sync", act, m_act_xn[tt - 2])
                        ld = p.op("sync", "dma_start", out=xt[:],
                                  in_=x_h[tt * 128:(tt + 1) * 128, :], inc=ldx, inck=16)
                        p.wait("scalar", ldx, ld)
                        if tt >= 1:
                            p.wait("scalar", dve, m_dve_stats[tt - 1])
                        a_sq = p.op("scalar", "activation", out=sq[:], in_=xt[:],
                                    func=AF.Square, inc=act)
                        p.wait("vector", ldx, ld)
                        if tt >= 2:
                            p.wait("vector", act, m_act_xn[tt - 2] - 1)
                        p.op("vector", "reduce_sum", out=stat[:, so + 0:so + 1], in_=xt[:], axis=AX.X)
                        p.wait("vector", act, a_sq)
                        p.op("vector", "reduce_sum", out=stat[:, so + 1:so + 2], in_=sq[:], axis=AX.X)
                        p.op("vector", "tensor_scalar", out=mean_sb[:, tt:tt + 1],
                             in0=stat[:, so + 0:so + 1], scalar1=1.0 / C, scalar2=None, op0=ALU.mult)
                        p.op("vector", "tensor_scalar", out=negmean_sb[:, tt:tt + 1],
                             in0=stat[:, so + 0:so + 1], scalar1=-1.0 / C, scalar2=None, op0=ALU.mult)
                        p.op("vector", "tensor_scalar", out=stat[:, so + 2:so + 3],
                             in0=stat[:, so + 1:so + 2], scalar1=1.0 / C, scalar2=EPS,
                             op0=ALU.mult, op1=ALU.add)
                        d_nv = p.op("vector", "scalar_tensor_tensor", out=stat[:, so + 3:so + 4],
                                    in0=mean_sb[:, tt:tt + 1], scalar=mean_sb[:, tt:tt + 1],
                                    in1=stat[:, so + 2:so + 3], op0=ALU.mult, op1=ALU.subtract,
                                    inc=dve)
                        m_dve_stats[tt] = d_nv
                        p.wait("scalar", dve, d_nv)
                        a_sd = p.op("scalar", "activation", out=stat[:, so + 4:so + 5],
                                    in_=stat[:, so + 3:so + 4], func=AF.Sqrt, scale=-1.0,
                                    inc=act)
                        p.wait("vector", act, a_sd)
                        p.op("vector", "reciprocal", out=rstd_sb[:, tt:tt + 1], in_=stat[:, so + 4:so + 5])
                        d_nb = p.op("vector", "tensor_scalar", out=stat[:, so + 5:so + 6],
                                    in0=rstd_sb[:, tt:tt + 1], scalar1=negmean_sb[:, tt:tt + 1],
                                    scalar2=None, op0=ALU.mult, inc=dve)
                        p.wait("scalar", dve, d_nb)
                        if tt >= 1:
                            p.wait("scalar", pe, m_pe_f32T[tt - 1])
                        a_xn = p.op("scalar", "activation", out=xn_bf[:], in_=xt[:], func=AF.Identity,
                                    bias=stat[:, so + 5:so + 6], scale=rstd_sb[:, tt:tt + 1], inc=act)
                        m_act_xn[tt] = a_xn
                        half = tt // (NT // 2)
                        trel = tt - half * (NT // 2)
                        p.wait("tensor", act, a_xn)
                        dstv = xnT[half][:].rearrange("p (ct t) -> p ct t", t=TH)
                        for b in range(2):
                            p.wait("tensor", dve)
                            for k in range(8):
                                ct = b * 8 + k
                                p.op("tensor", "transpose", out=psA_t[:, k * 128:(k + 1) * 128],
                                     in_=xn_bf[:, ct * 128:(ct + 1) * 128], identity=idb_sb[:],
                                     inc=(pe if k == 7 else None))
                            p.wait("vector", pe)
                            p.op("vector", "tensor_copy",
                                 out=dstv[:, b * 8:(b + 1) * 8, trel * 128:(trel + 1) * 128],
                                 in_=psA_t[:].rearrange("p (k t) -> p k t", t=128), inc=dve)
                        if tt >= 1:
                            p.wait("vector", pe, m_pe_raw[tt - 1])
                        for b in range(4):
                            p.wait("tensor", dve)
                            for k in range(4):
                                ct = b * 4 + k
                                p.op("tensor", "transpose", out=psA_f[:, k * 128:(k + 1) * 128],
                                     in_=xt[:, ct * 128:(ct + 1) * 128], identity=idf_sb[:],
                                     inc=(pe if k == 3 else None))
                            if b == 3:
                                m_pe_f32T[tt] = pe.n
                            p.wait("vector", pe)
                            p.op("vector", "tensor_copy", out=xT[:, b * 512:(b + 1) * 512],
                                 in_=psA_f[:], inc=dve)
                        p.wait("tensor", dve)
                        for ct in range(NC):
                            p.op("tensor", "matmul", out=psA_r[:],
                                 lhsT=gwT_sb[:, ct * 64:(ct + 1) * 64],
                                 rhs=xT[:, ct * 128:(ct + 1) * 128],
                                 start=(ct == 0), stop=(ct == NC - 1),
                                 inc=(pe if ct == NC - 1 else None))
                        m_pe_raw[tt] = pe.n
                        p.wait("vector", pe, m_pe_raw[tt])
                        p.op("vector", "tensor_copy", out=raw_all[:, tt * 128:(tt + 1) * 128],
                             in_=psA_r[:], inc=dve)
                        p.wait("tensor", dve)

                    SCL = PH_SB("scl", (128, 4 * NT))
                    el_all = PH_SB("el_all", (128, NT * 64))
                    e8_all = PH_SB("e8_all", (128, NT * 8))
                    g64 = PH_SB("g64", (128, NT * 64))
                    wtile = PH_SB("wtile", (128, 2 * 64))
                    p.wait("tensor", dve)
                    for tt in range(NT):
                        pt = p.op("tensor", "transpose", out=psA_w[:],
                                  in_=raw_all[0:64, tt * 128:(tt + 1) * 128],
                                  identity=idf_sb[0:64, 0:64], inc=pe)
                        p.wait("vector", pe, pt)
                        p.op("vector", "scalar_tensor_tensor", out=s64[:], in0=Sb_sb[:],
                             scalar=negmean_sb[:, tt:tt + 1], in1=psA_w[:],
                             op0=ALU.mult, op1=ALU.add, inc=dve)
                        p.wait("tensor", dve)
                        p.op("vector", "tensor_scalar", out=logits_all[:, tt * 64:(tt + 1) * 64],
                             in0=s64[:], scalar1=rstd_sb[:, tt:tt + 1], scalar2=None, op0=ALU.mult)
                    for tt in range(NT):
                        p.op("vector", "max", out=top8_all[:, tt * 8:(tt + 1) * 8],
                             in_=logits_all[:, tt * 64:(tt + 1) * 64])
                    t8v = top8_all[:].rearrange("p (t k) -> p t k", k=8)
                    sclv = SCL[:].rearrange("p (g t) -> p g t", t=NT)
                    p.op("vector", "tensor_scalar", out=sclv[:, 0:1, :].rearrange("p g t -> p (g t)"),
                         in0=t8v[:, :, 0:1].rearrange("p t k -> p (t k)"),
                         scalar1=-1.0, scalar2=None, op0=ALU.mult)
                    d_p2 = p.op("vector", "tensor_copy",
                                out=sclv[:, 1:2, :].rearrange("p g t -> p (g t)"),
                                in_=t8v[:, :, 7:8].rearrange("p t k -> p (t k)"), inc=dve)
                    p.wait("scalar", dve, d_p2)
                    for tt in range(NT):
                        p.op("scalar", "activation", out=e8_all[:, tt * 8:(tt + 1) * 8],
                             in_=top8_all[:, tt * 8:(tt + 1) * 8], func=AF.Exp,
                             bias=SCL[:, tt:tt + 1])
                        a_el = p.op("scalar", "activation", out=el_all[:, tt * 64:(tt + 1) * 64],
                                    in_=logits_all[:, tt * 64:(tt + 1) * 64], func=AF.Exp,
                                    bias=SCL[:, tt:tt + 1], inc=act)
                    p.wait("vector", act, act.n)
                    for tt in range(NT):
                        p.op("vector", "reduce_sum", out=SCL[:, 2 * NT + tt:2 * NT + tt + 1],
                             in_=e8_all[:, tt * 8:(tt + 1) * 8], axis=AX.X)
                    for tt in range(NT):
                        p.op("vector", "tensor_single_scalar", out=g64[:, tt * 64:(tt + 1) * 64],
                             in_=logits_all[:, tt * 64:(tt + 1) * 64],
                             scalar=SCL[:, NT + tt:NT + tt + 1], op=ALU.is_ge)
                    p.op("vector", "reciprocal", out=SCL[:, 3 * NT:4 * NT], in_=SCL[:, 2 * NT:3 * NT])
                    for tt in range(NT):
                        p.op("vector", "tensor_tensor", out=el_all[:, tt * 64:(tt + 1) * 64],
                             in0=el_all[:, tt * 64:(tt + 1) * 64],
                             in1=g64[:, tt * 64:(tt + 1) * 64], op=ALU.mult)
                    m_pe_wT = [0] * NT
                    for tt in range(NT):
                        wt_buf = wtile[:, (tt % 2) * 64:(tt % 2) * 64 + 64]
                        if tt >= 2:
                            p.wait("vector", pe, m_pe_wT[tt - 2])
                        d_w = p.op("vector", "tensor_scalar", out=wt_buf,
                                   in0=el_all[:, tt * 64:(tt + 1) * 64],
                                   scalar1=SCL[:, 3 * NT + tt:3 * NT + tt + 1], scalar2=None,
                                   op0=ALU.mult, inc=dve)
                        p.wait("tensor", dve, d_w)
                        m_pe_wT[tt] = p.op("tensor", "transpose", out=psA_w2[:], in_=wt_buf,
                                           identity=idf_sb[:], inc=pe)
                        p.wait("vector", pe, m_pe_wT[tt])
                        p.op("vector", "tensor_copy", out=WT_sb[:, tt * 128:(tt + 1) * 128],
                             in_=psA_w2[:], inc=dve)
                        p.wait("tensor", dve)

                    if debug:
                        p.wait("sync", dve)
                    dbg_dump(p, "xnT1", xnT1_sb[:], (128, NC * TH), BF16)
                    dbg_dump(p, "xnT2", xnT2_sb[:], (128, NC * TH), BF16)
                    dbg_dump(p, "WT", WT_sb[:], (64, T), F32)
                    dbg_dump(p, "raw", raw_all[:], (64, NT * 128), F32)
                    dbg_dump(p, "logits", logits_all[:], (128, NT * 64), F32)
                    dbg_dump(p, "rawT", rawT_all[:], (128, NT * 64), F32)
                    dbg_dump(p, "top8", top8_all[:], (128, NT * 8), F32)
                    dbg_dump(p, "Sb", Sb_sb[:], (128, NE), F32)
                    dbg_dump(p, "mean", mean_sb[:], (128, NT), F32)
                    dbg_dump(p, "rstd", rstd_sb[:], (128, NT), F32)
                    p.emit()

            sa_scope.close()

            bcd = contextlib.ExitStack()
            if "B" in phases:
                BS = lambda n, s, d=F32: bcd.enter_context(nc.sbuf_tensor(n, list(s), d))
                kT_sb = BS("kT", (128, KV * T), BF16)
                v_sb = BS("v", (128, NT * 512), BF16)
                with contextlib.ExitStack() as ph:
                    psB_t = ph.enter_context(nc.psum_tensor("psB_t", [128, 1024], BF16))
                    psB_k0 = ph.enter_context(nc.psum_tensor("psB_k0", [128, 512], F32))
                    psB_k1 = ph.enter_context(nc.psum_tensor("psB_k1", [128, 512], F32))
                    wn_sb = ph.enter_context(nc.sbuf_tensor("wn", [128, 4 * C], BF16))
                    wkT_sb = ph.enter_context(nc.sbuf_tensor("wkT", [128, NC * 512], BF16))
                    wvT_sb = ph.enter_context(nc.sbuf_tensor("wvT", [128, NC * 512], BF16))
                    kf_sb = ph.enter_context(nc.sbuf_tensor("kf", [128, 2 * 512], F32))
                    tmp_sb = ph.enter_context(nc.sbuf_tensor("tmpB", [128, 512], F32))
                    psk = (psB_k0, psB_k1)

                    def load_transpose_w(src_2d, dst_sb, nrow_tiles):
                        nrt = nrow_tiles
                        p.wait("gpsimd", pe)
                        for gi in range(nrt):
                            ldv = p.op("gpsimd", "dma_start",
                                       out=wn_sb[:, gi * C:(gi + 1) * C],
                                       in_=src_2d[gi * 128:(gi + 1) * 128, :],
                                       inc=ldw, inck=16)
                        p.wait("tensor", ldw, ldv)
                        dstv = dst_sb[:].rearrange("p (ct r) -> p ct r", r=nrt * 128)
                        for gi in range(nrt):
                            for cb in range(2):
                                p.wait("tensor", dve)
                                for k in range(8):
                                    ct = cb * 8 + k
                                    p.op("tensor", "transpose",
                                         out=psB_t[:, k * 128:(k + 1) * 128],
                                         in_=wn_sb[:, gi * C + ct * 128:gi * C + (ct + 1) * 128],
                                         identity=idb_sb[:], inc=(pe if k == 7 else None))
                                p.wait("vector", pe)
                                p.op("vector", "tensor_copy",
                                     out=dstv[:, cb * 8:(cb + 1) * 8, gi * 128:(gi + 1) * 128],
                                     in_=psB_t[:].rearrange("p (k t) -> p k t", t=128), inc=dve)

                    load_transpose_w(wk_h, wkT_sb, 4)
                    m_wkT = dve.n
                    load_transpose_w(wv_h, wvT_sb, 4)
                    m_wvT = dve.n

                    p.wait("tensor", dve, m_wkT)
                    m_rope = {}
                    for gi in range(KV):
                        for ncg in range(4):
                            t0 = ncg * 512
                            gidx = gi * 4 + ncg
                            if gidx >= 2:
                                p.wait("tensor", act, m_rope.get(("a", gidx - 2), 0))
                            for ct in range(NC):
                                p.op("tensor", "matmul", out=psk[gidx % 2][:],
                                     lhsT=wkT_sb[:, ct * 512 + gi * 128:ct * 512 + (gi + 1) * 128],
                                     rhs=xnT[t0 // TH][:, ct * TH + (t0 % TH):ct * TH + (t0 % TH) + 512],
                                     start=(ct == 0), stop=(ct == NC - 1),
                                     inc=(pe if ct == NC - 1 else None))
                            mk = pe.n
                            p.wait("scalar", pe, mk)
                            if gidx >= 2:
                                p.wait("scalar", dve, m_rope.get(("d", gidx - 2), 0))
                            kf = kf_sb[:, (gidx % 2) * 512:(gidx % 2) * 512 + 512]
                            m_rope[("a", gidx)] = p.op("scalar", "activation", out=kf,
                                                       in_=psk[gidx % 2][:], func=AF.Identity,
                                                       bias=bk_sb[:, gi:gi + 1], inc=act)
                            p.wait("vector", act, m_rope[("a", gidx)])
                            p.op("vector", "tensor_copy", out=tmp_sb[0:64, :], in_=kf.tensor[64:128, (gidx % 2) * 512:(gidx % 2) * 512 + 512])
                            p.op("vector", "tensor_copy", out=tmp_sb[64:128, :], in_=kf.tensor[0:64, (gidx % 2) * 512:(gidx % 2) * 512 + 512])
                            p.op("vector", "tensor_scalar", out=kf, in0=kf,
                                 scalar1=ck_sb[:, gi:gi + 1], scalar2=None, op0=ALU.mult)
                            m_rope[("d", gidx)] = p.op("vector", "scalar_tensor_tensor",
                                                       out=kT_sb[:, gi * T + t0:gi * T + t0 + 512],
                                                       in0=tmp_sb[:], scalar=ssk_sb[:, gi:gi + 1],
                                                       in1=kf, op0=ALU.mult, op1=ALU.add, inc=dve)

                    p.wait("tensor", dve, m_wvT)
                    p.wait("tensor", act, m_rope[("a", 15)])
                    m_vev = {}
                    for tt in range(NT):
                        if tt >= 2:
                            p.wait("tensor", dve, m_vev.get(tt - 2, 0))
                        half = tt // (NT // 2)
                        trel = tt - half * (NT // 2)
                        for ct in range(NC):
                            p.op("tensor", "matmul", out=psk[tt % 2][:],
                                 lhsT=xnT[half][:, ct * TH + trel * 128:ct * TH + (trel + 1) * 128],
                                 rhs=wvT_sb[:, ct * 512:(ct + 1) * 512],
                                 start=(ct == 0), stop=(ct == NC - 1),
                                 inc=(pe if ct == NC - 1 else None))
                            mv = pe.n
                        p.wait("vector", pe, mv)
                        m_vev[tt] = p.op("vector", "tensor_tensor", out=v_sb[:, tt * 512:(tt + 1) * 512],
                                         in0=psk[tt % 2][:], in1=bvb_sb[:], op=ALU.add, inc=dve)

                    if debug:
                        p.wait("sync", dve)
                    dbg_dump(p, "kT", kT_sb[:], (128, KV * T), BF16)
                    dbg_dump(p, "v", v_sb[:], (128, NT * 512), BF16)
                    p.emit()

            if "C" in phases:
                qT_all = bcd.enter_context(nc.sbuf_tensor("qT_all", [128, NH * T], BF16))
                with contextlib.ExitStack() as ph:
                    PH_PS = lambda n, s, d=F32: ph.enter_context(nc.psum_tensor(n, list(s), d))
                    PH_SB = lambda n, s, d=F32: ph.enter_context(nc.sbuf_tensor(n, list(s), d))
                    psC_t = PH_PS("psC_t", (128, 1024), BF16)
                    psC_s0 = PH_PS("psC_s0", (128, 512))
                    psC_s1 = PH_PS("psC_s1", (128, 512))
                    wqn_sb = PH_SB("wqn", (128, 2 * C), BF16)
                    wqT_sb = PH_SB("wqT", (128, 2 * C), BF16)
                    qf_sb = PH_SB("qf", (128, 2 * 512))
                    qtmp_sb = PH_SB("qtmp", (128, 512))
                    pss = (psC_s0, psC_s1)
                    m_wq_pe = [0] * NH
                    m_q_pe = [0] * NH
                    m_aq = 0
                    for h in range(NH):
                        par = h % 2
                        if h >= 2:
                            p.wait("gpsimd", pe, m_wq_pe[h - 2])
                        lq = p.op("gpsimd", "dma_start", out=wqn_sb[:, par * C:(par + 1) * C],
                                  in_=wq_h[h * 128:(h + 1) * 128, :], inc=ldw, inck=16)
                        p.wait("tensor", ldw, lq)
                        if h >= 2:
                            p.wait("tensor", pe, m_q_pe[h - 2])
                        wqT = wqT_sb[:, par * C:(par + 1) * C]
                        for cb in range(2):
                            p.wait("tensor", dve)
                            for k in range(8):
                                ct = cb * 8 + k
                                p.op("tensor", "transpose", out=psC_t[:, k * 128:(k + 1) * 128],
                                     in_=wqn_sb[:, par * C + ct * 128:par * C + (ct + 1) * 128],
                                     identity=idb_sb[:], inc=(pe if k == 7 else None))
                            p.wait("vector", pe)
                            p.op("vector", "tensor_copy",
                                 out=wqT.rearrange("p (cb2 r) -> p cb2 r", r=128)[:, cb * 8:(cb + 1) * 8, :],
                                 in_=psC_t[:].rearrange("p (k t) -> p k t", t=128), inc=dve)
                        m_wq_pe[h] = pe.n
                        p.wait("tensor", dve)
                        for ncg in range(4):
                            qpar = ncg % 2
                            t0 = ncg * 512
                            if h + ncg > 0:
                                p.wait("tensor", dve)
                            for ct in range(NC):
                                p.op("tensor", "matmul", out=pss[qpar][:],
                                     lhsT=wqT[:, ct * 128:(ct + 1) * 128],
                                     rhs=xnT[t0 // TH][:, ct * TH + (t0 % TH):ct * TH + (t0 % TH) + 512],
                                     start=(ct == 0), stop=(ct == NC - 1),
                                     inc=(pe if ct == NC - 1 else None))
                            mq = pe.n
                            p.wait("scalar", pe, mq)
                            qf = qf_sb[:, qpar * 512:(qpar + 1) * 512]
                            m_aq = p.op("scalar", "activation", out=qf, in_=pss[qpar][:],
                                        func=AF.Identity, bias=bq_sb[:, h:h + 1], inc=act)
                            p.wait("vector", act, m_aq)
                            p.op("vector", "tensor_copy", out=qtmp_sb[0:64, :],
                                 in_=qf_sb[64:128, qpar * 512:(qpar + 1) * 512])
                            p.op("vector", "tensor_copy", out=qtmp_sb[64:128, :],
                                 in_=qf_sb[0:64, qpar * 512:(qpar + 1) * 512])
                            p.op("vector", "tensor_scalar", out=qf, in0=qf,
                                 scalar1=cq_sb[:, h:h + 1], scalar2=None, op0=ALU.mult)
                            p.op("vector", "scalar_tensor_tensor",
                                 out=qT_all[:, h * T + t0:h * T + t0 + 512],
                                 in0=qtmp_sb[:], scalar=ssq_sb[:, h:h + 1],
                                 in1=qf, op0=ALU.mult, op1=ALU.add, inc=dve)
                        m_q_pe[h] = pe.n
                p.wait("sync", dve)
                sp1 = p.op("sync", "dma_start", out=xn_dram[:, 0:NC * TH], in_=xnT1_sb[:],
                           inc=sto, inck=16)
                sp2 = p.op("sync", "dma_start", out=xn_dram[:, NC * TH:], in_=xnT2_sb[:],
                           inc=sto, inck=16)
                p.emit()
                xn_scope.close()

                ctxT_sb = bcd.enter_context(nc.sbuf_tensor("ctxT", [128, NH * T], BF16))
                with contextlib.ExitStack() as ph:
                    PH_PS = lambda n, s, d=F32: ph.enter_context(nc.psum_tensor(n, list(s), d))
                    PH_SB = lambda n, s, d=F32: ph.enter_context(nc.sbuf_tensor(n, list(s), d))
                    psC_s0 = PH_PS("psC2_s0", (128, 512))
                    psC_s1 = PH_PS("psC2_s1", (128, 512))
                    psC_o = PH_PS("psC_o", (128, 512))
                    psC_d = PH_PS("psC_d", (1, 512))
                    psC_b = PH_PS("psC_b", (128, 512))
                    maskd_sb = PH_SB("maskd", (128, 4 * 512))
                    exp_sb = PH_SB("expS", (128, NT * 512), BF16)
                    den_sb = PH_SB("den", (1, 2 * 512))
                    rden_sb = PH_SB("rden", (1, 2 * 512))
                    rdb_sb = PH_SB("rdb", (128, 512))
                    pss = (psC_s0, psC_s1)
                    lm = p.op("sync", "dma_start", out=maskd_sb[:], in_=K["maskdiag"],
                              inc=ldx, inck=16)
                    p.wait("vector", ldx, lm)
                    p.wait("vector", sto, sp2)
                    p.wait("tensor", dve)
                    exp_hist = []
                    m_av = {}
                    m_ctx = 0
                    m_ad = 0
                    m_dr = 0
                    for h in range(NH):
                        g = h // 4
                        for tc in range(4):
                            for st in range(NT):
                                spar = len(exp_hist) % 2
                                if len(exp_hist) >= 2:
                                    p.wait("tensor", act, exp_hist[-2])
                                elif h == 0 and tc == 0 and st == 0:
                                    p.wait("tensor", act, m_aq)
                                ms = p.op("tensor", "matmul", out=pss[spar][:],
                                          lhsT=kT_sb[:, g * T + st * 128:g * T + (st + 1) * 128],
                                          rhs=qT_all[:, h * T + tc * 512:h * T + (tc + 1) * 512],
                                          start=True, stop=True, inc=pe)
                                r = st - 4 * tc
                                if 0 <= r <= 3:
                                    p.wait("vector", pe, ms)
                                    dm = p.op("vector", "tensor_tensor", out=pss[spar][:],
                                              in0=pss[spar][:], in1=maskd_sb[:, r * 512:(r + 1) * 512],
                                              op=ALU.add, inc=dve)
                                    p.wait("scalar", dve, dm)
                                else:
                                    p.wait("scalar", pe, ms)
                                if (h, tc) != (0, 0) and st == 0:
                                    p.wait("scalar", pe, m_av[(h, tc - 1) if tc > 0 else (h - 1, 3)])
                                bias_ap = onecol_sb[:] if st < 4 * tc else 0.0
                                e_ms = p.op("scalar", "activation",
                                            out=exp_sb[:, st * 512:(st + 1) * 512],
                                            in_=pss[spar][:], func=AF.Exp,
                                            bias=bias_ap, inc=act)
                                exp_hist.append(e_ms)
                            p.wait("tensor", act, exp_hist[-1])
                            if (h, tc) != (0, 0):
                                p.wait("tensor", act, m_ad)
                                p.wait("tensor", dve, m_ctx)
                            for st in range(NT):
                                p.op("tensor", "matmul", out=psC_d[:],
                                     lhsT=onesb_col_sb[:], rhs=exp_sb[:, st * 512:(st + 1) * 512],
                                     start=(st == 0), stop=(st == NT - 1),
                                     inc=(pe if st == NT - 1 else None))
                            m_denv = pe.n
                            for st in range(NT):
                                p.op("tensor", "matmul", out=psC_o[:],
                                     lhsT=v_sb[:, st * 512 + g * 128:st * 512 + (g + 1) * 128],
                                     rhs=exp_sb[:, st * 512:(st + 1) * 512],
                                     start=(st == 0), stop=(st == NT - 1),
                                     inc=(pe if st == NT - 1 else None))
                            m_av[(h, tc)] = pe.n
                            dpar = (4 * h + tc) % 2
                            p.wait("scalar", pe, m_denv)
                            p.wait("scalar", dve, m_dr)
                            m_ad = p.op("scalar", "copy", out=den_sb[0:1, dpar * 512:(dpar + 1) * 512],
                                        in_=psC_d[:], inc=act)
                            p.wait("vector", act, m_ad)
                            m_dr = p.op("vector", "reciprocal",
                                        out=rden_sb[0:1, dpar * 512:(dpar + 1) * 512],
                                        in_=den_sb[0:1, dpar * 512:(dpar + 1) * 512], inc=dve)
                            p.wait("tensor", dve, m_dr)
                            pb = p.op("tensor", "matmul", out=psC_b[:], lhsT=ones_row_sb[:],
                                      rhs=rden_sb[0:1, dpar * 512:(dpar + 1) * 512],
                                      start=True, stop=True, inc=pe)
                            p.wait("vector", pe, pb)
                            p.op("vector", "tensor_copy", out=rdb_sb[:], in_=psC_b[:], inc=dve)
                            p.wait("vector", pe, m_av[(h, tc)])
                            m_ctx = p.op("vector", "tensor_tensor",
                                         out=ctxT_sb[:, h * T + tc * 512:h * T + (tc + 1) * 512],
                                         in0=psC_o[:], in1=rdb_sb[:], op=ALU.mult, inc=dve)
                    if debug:
                        p.wait("sync", dve)
                    dbg_dump(p, "ctxT", ctxT_sb[:], (128, NH * T), BF16)
                    p.emit()

            if "D" in phases:
                with contextlib.ExitStack() as ph:
                    psD_t = ph.enter_context(nc.psum_tensor("psD_t", [128, 1024], BF16))
                    psD_a = ph.enter_context(nc.psum_tensor("psD_a", [128, 512], F32))
                    psD_b = ph.enter_context(nc.psum_tensor("psD_b", [128, 512], F32))
                    won_sb = ph.enter_context(nc.sbuf_tensor("won", [128, 2 * C], BF16))
                    woT_sb = ph.enter_context(nc.sbuf_tensor("woT", [128, 2 * C], BF16))
                    ast_sb = ph.enter_context(nc.sbuf_tensor("ast", [128, 2 * T], BF16))
                    psd = (psD_a, psD_b)
                    p.wait("gpsimd", pe)
                    p.wait("gpsimd", dve)
                    m_wo_pe = [0] * NC
                    m_out_pe = [0] * NC
                    m_ast = [0] * NC
                    m_aev = {}
                    for jo in range(NC):
                        par = jo % 2
                        if jo >= 2:
                            p.wait("gpsimd", pe, m_wo_pe[jo - 2])
                        lw = p.op("gpsimd", "dma_start", out=won_sb[:, par * C:(par + 1) * C],
                                  in_=wo_h[jo * 128:(jo + 1) * 128, :], inc=ldw, inck=16)
                        p.wait("tensor", ldw, lw)
                        if jo >= 2:
                            p.wait("tensor", pe, m_out_pe[jo - 2])
                        woT = woT_sb[:, par * C:(par + 1) * C]
                        for cb in range(2):
                            p.wait("tensor", dve)
                            for k in range(8):
                                jt = cb * 8 + k
                                p.op("tensor", "transpose", out=psD_t[:, k * 128:(k + 1) * 128],
                                     in_=won_sb[:, par * C + jt * 128:par * C + (jt + 1) * 128],
                                     identity=idb_sb[:], inc=(pe if k == 7 else None))
                            p.wait("vector", pe)
                            p.op("vector", "tensor_copy",
                                 out=woT.rearrange("p (cb2 r) -> p cb2 r", r=128)[:, cb * 8:(cb + 1) * 8, :],
                                 in_=psD_t[:].rearrange("p (k t) -> p k t", t=128), inc=dve)
                        m_wo_pe[jo] = pe.n
                        p.wait("tensor", dve)
                        if jo >= 2:
                            p.wait("scalar", sto, m_ast[jo - 2])
                        for tc in range(4):
                            tpar = tc % 2
                            if jo + tc > 0:
                                p.wait("tensor", act, m_aev.get((jo, tc - 2),
                                                               m_aev.get((jo - 1, 2 + tc), 0)))
                            for jt in range(NC):
                                p.op("tensor", "matmul", out=psd[tpar][:],
                                     lhsT=woT[:, jt * 128:(jt + 1) * 128],
                                     rhs=ctxT_sb[:, jt * T + tc * 512:jt * T + (tc + 1) * 512],
                                     start=(jt == 0), stop=(jt == NC - 1),
                                     inc=(pe if jt == NC - 1 else None))
                            mo = pe.n
                            p.wait("scalar", pe, mo)
                            m_aev[(jo, tc)] = p.op("scalar", "activation",
                                                   out=ast_sb[:, par * T + tc * 512:par * T + (tc + 1) * 512],
                                                   in_=psd[tpar][:], func=AF.Identity,
                                                   bias=bo_sb[:, jo:jo + 1], inc=act)
                        m_out_pe[jo] = pe.n
                        p.wait("sync", act, m_aev[(jo, 3)])
                        m_ast[jo] = p.op("sync", "dma_start",
                                         out=attn_dram[:, jo * T:(jo + 1) * T],
                                         in_=ast_sb[:, par * T:(par + 1) * T], inc=sto, inck=16)
                    p.wait("sync", sto)
                    dbg_dump(p, "attn", attn_dram, (128, NC * T), BF16)
                    p.emit()

            if "EF" in phases:
                bcd.close()
                guwT_dramA = nc.dram_tensor("guwT_dramA", [32 * 8 * 128, C], BF16,
                                            kind="Internal").ap()
                guwT_dramB = nc.dram_tensor("guwT_dramB", [33 * 8 * 128, C], BF16,
                                            kind="Internal").ap()
                dpwT_dram = nc.dram_tensor("dpwT_dram", [(NE + 1) * 128, 4 * C], BF16,
                                           kind="Internal").ap()

                def guwT_dram_slice(eidx, si):
                    if eidx < 32:
                        r = (eidx * 8 + si) * 128
                        return guwT_dramA[r:r + 128, :]
                    r = ((eidx - 32) * 8 + si) * 128
                    return guwT_dramB[r:r + 128, :]
                ef = contextlib.ExitStack()
                EFS = lambda n, s, d=F32: ef.enter_context(nc.sbuf_tensor(n, list(s), d))
                xnh_sb = EFS("xnh", (128, NC * TH), BF16)
                acc_sb = EFS("acc", (128, NC * TH))
                p.wait("gpsimd", pe)
                p.wait("gpsimd", dve)
                p.wait("gpsimd", act)
                p.wait("sync", pe)
                p.wait("sync", dve)
                for hf in range(2):
                    with contextlib.ExitStack() as eph:
                        PS2 = lambda n, s, d=F32: eph.enter_context(nc.psum_tensor(n + str(hf), list(s), d))
                        SB2 = lambda n, s, d=F32: eph.enter_context(nc.sbuf_tensor(n + str(hf), list(s), d))
                        ps_g = PS2("ps_g", (128, 1024))
                        ps_u = PS2("ps_u", (128, 1024))
                        ps_d0 = PS2("ps_d0", (128, 512))
                        ps_d1 = PS2("ps_d1", (128, 512))
                        if hf == 0:
                            ps_t0 = PS2("ps_t0", (128, 1024), BF16)
                            ps_t1 = PS2("ps_t1", (128, 1024), BF16)
                        if hf == 0:
                            gu_nat = SB2("gu_nat", (128, 2 * C), BF16)
                            guwT = SB2("guwT", (128, 2 * C), BF16)
                            dp_nat = SB2("dp_nat", (128, NC * 512), BF16)
                            ps_u2 = ps_u
                        else:
                            guwT4 = SB2("guwT4", (128, 4 * C), BF16)
                            ps_u2 = PS2("ps_u2", (128, 1024))
                        dpwT = SB2("dpwT", (128, 4 * C), BF16)
                        interT = SB2("interT", (128, 4 * TH), BF16)
                        sg_sb = SB2("sg", (128, 1024))
                        p1_sb = sg_sb
                        Wb_sb = SB2("Wb", (128, 2 * TH))
                        wrow_sb = SB2("wrow", (1, 2 * TH))
                        st_misc2 = None
                        shrow_sb = wrow_sb
                        psd = (ps_d0, ps_d1)
                        pst = (ps_t0, ps_t1) if hf == 0 else None

                        if hf == 1:
                            p.wait("gpsimd", dve)
                            p.wait("gpsimd", pe)
                            p.wait("sync", pe)
                            p.wait("sync", dve)
                        rs = p.op("sync", "dma_start", out=xnh_sb[:],
                                  in_=xn_dram[:, hf * NC * TH:(hf + 1) * NC * TH],
                                  inc=ldx, inck=16)
                        for jo in range(NC):
                            p.op("gpsimd", "dma_start",
                                 out=acc_sb[:, jo * TH:(jo + 1) * TH],
                                 in_=attn_dram[:, jo * T + hf * TH:jo * T + hf * TH + TH],
                                 inc=ldw, inck=16)
                        m_acc_init = ldw.n
                        m_xnh = rs

                        st_dve = {"wb": 0, "intT": [0] * 4, "accl": 0, "accl2": 0}
                        st_pe = {"dp_tr": 0, "up": [0] * 8, "dn": 0,
                                 "wb": 0, "gu_done": [0] * 8}
                        tr_hist = []
                        spill_gu = [0] * 8
                        spill_dp = [0]
                        st_misc = {"dp_ldw": 0, "wrow_ld": 0, "unit": 0, "wb_unit": 1}
                        acc_hist = []
                        order = (0, 4, 1, 5, 2, 6, 3, 7)

                        def tr_batch(srcs, dst_ap):
                            b = pst[len(tr_hist) % 2]
                            if len(tr_hist) >= 2:
                                p.wait("tensor", dve, tr_hist[-2])
                            for k, s_ap in enumerate(srcs):
                                p.op("tensor", "transpose", out=b[:, k * 128:(k + 1) * 128],
                                     in_=s_ap, identity=idb_sb[:],
                                     inc=(pe if k == 7 else None))
                            p.wait("vector", pe)
                            ev = p.op("vector", "tensor_copy", out=dst_ap,
                                      in_=b[:].rearrange("p (k t) -> p k t", t=128), inc=dve)
                            tr_hist.append(ev)
                            return ev

                        def up_pair_down(eidx, acquire_gu, acquire_dp, post_up=None,
                                         next_wb=None, gu_evac_ms=None, gu_lhsT=None):
                            if gu_lhsT is None:
                                gu_lhsT = lambda si, ct: guwT[:, (si % 2) * C + ct * 128:(si % 2) * C + (ct + 1) * 128]
                            wbpar = st_misc["unit"] % 2
                            st_misc["unit"] += 1
                            """Shared core: acquire weights (callbacks), up matmuls in
                            pair order, silu/mul/Wb -> interT, down matmuls -> acc.
                            gu_evac_ms(si) -> (first_half_ms, second_half_ms) dve
                            milestones for the slice's transposed-weight halves
                            (None = gate on ldw only, handled by acquire_gu)."""
                            acquire_dp()
                            acquire_gu(0)
                            for si, s in enumerate(order):
                                par = si % 2
                                if si + 1 < 8:
                                    acquire_gu(si + 1)
                                ev = gu_evac_ms(si) if gu_evac_ms else (0, 0)
                                j = s % 4
                                psu_j = ps_u if (j % 2 == 0 or ps_u2 is ps_u) else ps_u2
                                dst = ps_g if si % 2 == 0 else psu_j
                                if si % 2 == 0:
                                    p.wait("tensor", act, act.n)
                                elif ps_u2 is ps_u:
                                    p.wait("tensor", dve, st_dve["intT"][j - 1] if j > 0 else st_dve["accl"])
                                else:
                                    p.wait("tensor", dve, st_dve["intT"][j - 2] if j >= 2 else st_dve["accl2"])
                                for half2 in range(2):
                                    if ev[half2]:
                                        p.wait("tensor", dve, ev[half2])
                                    for tc in range(2):
                                        for ct in range(half2 * 8, half2 * 8 + 8):
                                            p.op("tensor", "matmul",
                                                 out=dst[:, tc * 512:(tc + 1) * 512],
                                                 lhsT=gu_lhsT(si, ct),
                                                 rhs=xnh_sb[:, ct * TH + tc * 512:ct * TH + (tc + 1) * 512],
                                                 start=(ct == 0), stop=(ct == NC - 1),
                                                 inc=(pe if ct == NC - 1 and tc == 1 else None))
                                st_pe["up"][si] = pe.n
                                if post_up is not None:
                                    post_up(si)
                                if si == 5 and next_wb is not None:
                                    next_wb()
                                if si % 2 == 1:
                                    p.wait("scalar", pe, st_pe["up"][si - 1])
                                    a_sg = p.op("scalar", "activation", out=sg_sb[:],
                                                in_=ps_g[:], func=AF.Silu, inc=act)
                                    p.wait("vector", act, a_sg)
                                    p.wait("vector", pe, st_pe["up"][si])
                                    p.op("vector", "tensor_tensor", out=p1_sb[:],
                                         in0=sg_sb[:], in1=psu_j[:], op=ALU.mult)
                                    st_dve["intT"][j] = p.op(
                                        "vector", "tensor_tensor",
                                        out=interT[:, j * TH:(j + 1) * TH],
                                        in0=p1_sb[:],
                                        in1=Wb_sb[:, wbpar * TH:(wbpar + 1) * TH],
                                        op=ALU.mult, inc=dve)
                            p.wait("tensor", dve, st_dve["intT"][3])
                            p.wait("tensor", ldw, st_misc["dp_ldw"])
                            for nacc in range(NC * 2):
                                ht, tc = nacc // 2, nacc % 2
                                dpar = nacc % 2
                                if nacc == 0:
                                    p.wait("tensor", dve, st_dve["wb"])
                                if len(acc_hist) >= 2:
                                    p.wait("tensor", dve, acc_hist[-2])
                                for j in range(4):
                                    p.op("tensor", "matmul", out=psd[dpar][:],
                                         lhsT=dpwT[:, j * C + ht * 128:j * C + (ht + 1) * 128],
                                         rhs=interT[:, j * TH + tc * 512:j * TH + (tc + 1) * 512],
                                         start=(j == 0), stop=(j == 3),
                                         inc=(pe if j == 3 else None))
                                mdn = pe.n
                                p.wait("vector", pe, mdn)
                                av = p.op("vector", "tensor_tensor",
                                          out=acc_sb[:, ht * TH + tc * 512:ht * TH + (tc + 1) * 512],
                                          in0=psd[dpar][:],
                                          in1=acc_sb[:, ht * TH + tc * 512:ht * TH + (tc + 1) * 512],
                                          op=ALU.add, inc=dve)
                                acc_hist.append(av)
                            st_pe["dn"] = pe.n
                            st_dve["accl2"] = st_dve["intT"][3]
                            st_dve["accl"] = acc_hist[-1]

                        def expert_h0(gu_srcs, dp_src, eidx, next_wb):
                            lgu = [0] * 8
                            ev_ms = [[0, 0] for _ in range(8)]

                            def gu_load(si):
                                p.wait("gpsimd", pe,
                                       st_pe["gu_done"][si - 2 if si >= 2 else si + 6])
                                lgu[si] = p.op("gpsimd", "dma_start",
                                               out=gu_nat[:, (si % 2) * C:(si % 2 + 1) * C],
                                               in_=gu_srcs[si], inc=ldw, inck=16)
                            gu_load(0)
                            gu_load(1)

                            def acquire_dp():
                                p.wait("gpsimd", pe, st_pe["dp_tr"])
                                for ht in range(NC):
                                    ldp = p.op("gpsimd", "dma_start",
                                               out=dp_nat[:, ht * 512:(ht + 1) * 512],
                                               in_=dp_src[ht * 128:(ht + 1) * 128, :],
                                               inc=ldw, inck=16)
                                p.wait("tensor", ldw, ldp)
                                dpv = dpwT[:].rearrange("p (j h) -> p j h", j=4)
                                for hb in range(8):
                                    j, hh = hb % 4, (hb // 4) * 8
                                    if hb == 0:
                                        p.wait("vector", sto, spill_dp[0])
                                    tr_batch([dp_nat[:, (hh + k) * 512 + j * 128:(hh + k) * 512 + (j + 1) * 128]
                                              for k in range(8)],
                                             dpv[:, j, hh * 128:(hh + 8) * 128])
                                st_pe["dp_tr"] = pe.n
                                p.wait("sync", dve, tr_hist[-1])
                                spill_dp[0] = p.op("sync", "dma_start",
                                                   out=dpwT_dram[eidx * 128:(eidx + 1) * 128, :],
                                                   in_=dpwT[:], inc=sto, inck=16)

                            def acquire_gu(si):
                                par = si % 2
                                p.wait("tensor", ldw, lgu[si])
                                guv = guwT[:].rearrange("p (q ct r) -> p q ct r", q=2, r=128)
                                for cb in range(2):
                                    if cb == 0:
                                        p.wait("vector", sto, spill_gu[si])
                                    tr_batch([gu_nat[:, par * C + (cb * 8 + k) * 128:par * C + (cb * 8 + k + 1) * 128]
                                              for k in range(8)],
                                             guv[:, par, cb * 8:(cb + 1) * 8, :])
                                    ev_ms[si][cb] = tr_hist[-1]
                                st_pe["gu_done"][si] = pe.n
                                if si + 2 < 8:
                                    gu_load(si + 2)
                                p.wait("sync", dve, tr_hist[-1])
                                sgu = p.op("sync", "dma_start",
                                           out=guwT_dram_slice(eidx, si),
                                           in_=guwT[:, par * C:(par + 1) * C], inc=sto, inck=16)
                                spill_gu[(si + 2) % 8] = sgu

                            up_pair_down(eidx, acquire_gu, acquire_dp,
                                         next_wb=next_wb,
                                         gu_evac_ms=lambda si: ev_ms[si])

                        def expert_h1(eidx, next_wb):
                            lgu = [0] * 8

                            def gu_load(si):
                                p.wait("gpsimd", pe,
                                       st_pe["up"][si - 4 if si >= 4 else si + 4])
                                lgu[si] = p.op("gpsimd", "dma_start",
                                               out=guwT4[:, (si % 4) * C:(si % 4 + 1) * C],
                                               in_=guwT_dram_slice(eidx, si),
                                               inc=ldw, inck=16)
                            for si0 in range(4):
                                gu_load(si0)

                            def acquire_dp():
                                p.wait("gpsimd", pe, st_pe["dn"])
                                st_misc["dp_ldw"] = p.op(
                                    "gpsimd", "dma_start", out=dpwT[:],
                                    in_=dpwT_dram[eidx * 128:(eidx + 1) * 128, :],
                                    inc=ldw, inck=16)

                            def acquire_gu(si):
                                p.wait("tensor", ldw, lgu[si])

                            def post_up(si):
                                if si + 4 < 8:
                                    gu_load(si + 4)

                            up_pair_down(eidx, acquire_gu, acquire_dp, post_up,
                                         next_wb=next_wb,
                                         gu_lhsT=lambda si, ct: guwT4[:, (si % 4) * C + ct * 128:(si % 4) * C + (ct + 1) * 128])

                        def wrow_load(e):
                            wpar = e % 2
                            p.wait("sync", pe, st_pe["wb"])
                            st_misc["wrow_ld"] = p.op(
                                "sync", "dma_start",
                                out=wrow_sb[0:1, wpar * TH:(wpar + 1) * TH],
                                in_=WT_sb[e:e + 1, hf * TH:(hf + 1) * TH],
                                inc=ldx, inck=16)

                        def wb_expert(e):
                            def f():
                                wpar = e % 2
                                wbp = st_misc["wb_unit"] % 2
                                st_misc["wb_unit"] += 1
                                p.wait("tensor", ldx, st_misc["wrow_ld"])
                                if acc_hist:
                                    p.wait("tensor", dve, acc_hist[-1])
                                p.wait("tensor", dve, st_dve["wb"])
                                for tc in range(2):
                                    p.op("tensor", "matmul", out=psd[tc][:],
                                         lhsT=ones_row_sb[:],
                                         rhs=wrow_sb[0:1, wpar * TH + tc * 512:wpar * TH + (tc + 1) * 512],
                                         start=True, stop=True,
                                         inc=(pe if tc == 1 else None))
                                mb = pe.n
                                p.wait("vector", pe, mb)
                                for tc in range(2):
                                    st_dve["wb"] = p.op(
                                        "vector", "tensor_copy",
                                        out=Wb_sb[:, wbp * TH + tc * 512:wbp * TH + (tc + 1) * 512],
                                        in_=psd[tc][:], inc=dve)
                                st_pe["wb"] = pe.n
                            return f

                        def wb_shared():
                            p.wait("tensor", ldx, m_xnh)
                            if acc_hist:
                                p.wait("tensor", dve, acc_hist[-1])
                            for tc in range(2):
                                p.wait("tensor", act, act.n)
                                p.wait("tensor", dve, st_dve["wb"])
                                for ct in range(NC):
                                    p.op("tensor", "matmul", out=psd[0][0:1, :],
                                         lhsT=g1T_sb[:, ct:ct + 1],
                                         rhs=xnh_sb[:, ct * TH + tc * 512:ct * TH + (tc + 1) * 512],
                                         start=(ct == 0), stop=(ct == NC - 1),
                                         inc=(pe if ct == NC - 1 else None))
                                mg = pe.n
                                p.wait("scalar", pe, mg)
                                ag = p.op("scalar", "activation",
                                          out=shrow_sb[0:1, TH + tc * 512:TH + (tc + 1) * 512],
                                          in_=psd[0][0:1, :], func=AF.Sigmoid, inc=act)
                                p.wait("tensor", act, ag)
                                mb = p.op("tensor", "matmul", out=psd[0][:],
                                          lhsT=ones_row_sb[:],
                                          rhs=shrow_sb[0:1, TH + tc * 512:TH + (tc + 1) * 512],
                                          start=True, stop=True, inc=pe)
                                p.wait("vector", pe, mb)
                                st_dve["wb"] = p.op("vector", "tensor_copy",
                                                    out=Wb_sb[:, tc * 512:(tc + 1) * 512],
                                                    in_=psd[0][:], inc=dve)
                            st_pe["wb"] = pe.n
                            st_misc["wb_unit"] = 1

                        p.wait("tensor", ldw, m_acc_init)
                        p.wait("tensor", ldx, m_xnh)
                        if hf == 0:
                            sh_gu = [shg_h[j * 128:(j + 1) * 128, :] if si % 2 == 0
                                     else shu_h[j * 128:(j + 1) * 128, :]
                                     for si, j in enumerate([s % 4 for s in order])]
                            wb_shared()
                            if n_experts:
                                wrow_load(0)
                            expert_h0(sh_gu, shd_h, NE,
                                      next_wb=(wb_expert(0) if n_experts else None))
                            for e in range(n_experts):
                                gu_srcs = [guw_h[e, s * 128:(s + 1) * 128, :] for s in order]
                                if e + 1 < n_experts:
                                    wrow_load(e + 1)
                                expert_h0(gu_srcs, dpw_h[e], e,
                                          next_wb=(wb_expert(e + 1) if e + 1 < n_experts else None))
                        else:
                            wb_shared()
                            if n_experts:
                                wrow_load(0)
                            expert_h1(NE, next_wb=(wb_expert(0) if n_experts else None))
                            for e in range(n_experts):
                                if e + 1 < n_experts:
                                    wrow_load(e + 1)
                                expert_h1(e, next_wb=(wb_expert(e + 1) if e + 1 < n_experts else None))

                        with contextlib.ExitStack() as fin:
                            xt_f = fin.enter_context(nc.sbuf_tensor("xt_f" + str(hf), [128, C], F32))
                            out_f = fin.enter_context(nc.sbuf_tensor("out_f" + str(hf), [128, C], F32))
                            psF = ps_g
                            m_fin_st = [0] * 8
                            m_fin_dve = [0] * 8
                            for tb in range(8):
                                if tb >= 1:
                                    p.wait("sync", dve, m_fin_dve[tb - 1])
                                lx = p.op("sync", "dma_start", out=xt_f[:],
                                          in_=x_h[hf * TH + tb * 128:hf * TH + (tb + 1) * 128, :],
                                          inc=ldx, inck=16)
                                for hb in range(2):
                                    if tb == 0 and hb == 0:
                                        p.wait("tensor", dve, acc_hist[-1])
                                        p.wait("tensor", act, act.n)
                                    else:
                                        p.wait("tensor", dve)
                                    for k in range(8):
                                        ht = hb * 8 + k
                                        p.op("tensor", "transpose",
                                             out=psF[:, k * 128:(k + 1) * 128],
                                             in_=acc_sb[:, ht * TH + tb * 128:ht * TH + (tb + 1) * 128],
                                             identity=idf_sb[:], inc=(pe if k == 7 else None))
                                    p.wait("vector", pe)
                                    p.wait("vector", ldx, lx)
                                    if tb >= 1 and hb == 0:
                                        p.wait("vector", sto, m_fin_st[tb - 1])
                                    m_fin_dve[tb] = p.op(
                                        "vector", "tensor_tensor",
                                        out=out_f[:, hb * 1024:(hb + 1) * 1024],
                                        in0=psF[:], in1=xt_f[:, hb * 1024:(hb + 1) * 1024],
                                        op=ALU.add, inc=dve)
                                p.wait("sync", dve, m_fin_dve[tb])
                                m_fin_st[tb] = p.op("sync", "dma_start",
                                                    out=out_h[hf * TH + tb * 128:hf * TH + (tb + 1) * 128, :],
                                                    in_=out_f[:],
                                                    inc=sto, inck=16)
                            p.emit()
                ef.close()
            else:
                bcd.close()

            p.wait("sync", sto)
            p.wait("sync", ldx)
            p.emit()

    return nc, dbg


def kernel(**inputs):
    nc, _ = build()
    wmap = {
        "wq": inputs["L0_attn_wq"], "bq": inputs["L0_attn_bq"],
        "wk": inputs["L0_attn_wk"], "bk": inputs["L0_attn_bk"],
        "wv": inputs["L0_attn_wv"], "bv": inputs["L0_attn_bv"],
        "wo": inputs["L0_attn_wo"], "bo": inputs["L0_attn_bo"],
        "gate_w": inputs["L0_ffn_gate_w"], "gu_w": inputs["L0_ffn_gu_w"],
        "dp_w": inputs["L0_ffn_dp_w"],
        "sh_gate_w": inputs["L0_ffn_sh_gate_w"], "sh_up_w": inputs["L0_ffn_sh_up_w"],
        "sh_down_w": inputs["L0_ffn_sh_down_w"], "sh_gate1_w": inputs["L0_ffn_sh_gate1_w"],
    }
    wmap = {k: np.ascontiguousarray(np.asarray(v, dtype=np.float32)) for k, v in wmap.items()}
    xfull = np.asarray(inputs["L0_attn_x"], dtype=np.float32)
    in_maps = [dict(wmap, x=np.ascontiguousarray(xfull[b])) for b in range(8)]
    res = run_bass_kernel_spmd(nc, in_maps, core_ids=list(range(8)))
    return np.stack([r["out"] for r in res.results], axis=0)


# version 08
# speedup vs baseline: 1.0679x; 1.0103x over previous
import contextlib
import numpy as np
import ml_dtypes
import concourse.bass as bass
import concourse.mybir as mybir
from concourse.bass_utils import run_bass_kernel_spmd

F32 = mybir.dt.float32
BF16 = mybir.dt.bfloat16
AF = mybir.ActivationFunctionType
ALU = mybir.AluOpType
AX = mybir.AxisListType
BF = ml_dtypes.bfloat16

T = 2048
C = 2048
NH = 16
KV = 4
NE = 64
NI = 1024
SI = 512
NT = T // 128
NC = C // 128
TH = T // 2
SCALE = 1.0 / np.sqrt(128.0)
EPS = 1e-5

ALL_PHASES = ("A", "B", "C", "D", "EF")


def _host_consts():
    j = np.arange(64)
    invf = 10000.0 ** (-(2.0 * j) / 128.0)
    dfreq = np.concatenate([invf, invf])
    cq = np.cos(np.outer(dfreq, np.arange(NH)))
    ssq = np.sin(np.outer(dfreq, np.arange(NH)))
    ssq[:64] *= -1.0
    ck = np.cos(np.outer(dfreq, np.arange(KV)))
    ssk = np.sin(np.outer(dfreq, np.arange(KV)))
    ssk[:64] *= -1.0
    u = np.arange(512)[None, :]
    pp = np.arange(128)[:, None]
    md = np.concatenate([(u >= (r * 128 + pp)).astype(np.float32) for r in range(4)], axis=1)
    return {
        "rope_cq": (cq * SCALE).astype(np.float32),
        "rope_ssq": (ssq * SCALE).astype(np.float32),
        "rope_ck": ck.astype(np.float32),
        "rope_ssk": ssk.astype(np.float32),
        "id_bf": np.eye(128, dtype=BF),
        "id_f32": np.eye(128, dtype=np.float32),
        "ones_f32_row": np.ones((1, 128), dtype=np.float32),
        "ones_f32_one": np.ones((1, 1), dtype=np.float32),
        "maskdiag": md,
    }


class Ctr:
    def __init__(self, stack, nc, name):
        self.h = stack.enter_context(nc.semaphore(name))
        self.n = 0
        self.name = name


class Prog:

    ENGS = ("sync", "gpsimd", "tensor", "vector", "scalar")

    def __init__(self, blk):
        self.blk = blk
        self.s = {e: [] for e in self.ENGS}

    def wait(self, eng, ctr, val=None):
        v = ctr.n if val is None else val
        if v <= 0:
            return
        self.s[eng].append(lambda E, h=ctr.h, v=v: E.wait_ge(h, v))

    def op(self, eng, fname, *args, inc=None, inck=1, **kw):
        ret = None
        if inc is not None:
            inc.n += inck
            ret = inc.n

        def f(E, fname=fname, args=args, kw=kw, inc=inc, inck=inck):
            inst = getattr(E, fname)(*args, **kw)
            if inc is not None:
                inst.then_inc(inc.h, inck)
        self.s[eng].append(f)
        return ret

    def emit(self):
        for eng in self.ENGS:
            stream = self.s[eng]
            if stream:
                getattr(self.blk, eng)(lambda E, stream=stream: [f(E) for f in stream])
        self.s = {e: [] for e in self.ENGS}


def build(phases=ALL_PHASES, debug=(), n_experts=NE):
    nc = bass.Bass("TRN2", target_bir_lowering=False)
    CST = _host_consts()

    x_h = nc.dram_tensor("x", [T, C], F32, kind="ExternalInput").ap()
    wq_h = nc.dram_tensor("wq", [C, C], F32, kind="ExternalInput").ap()
    bq_h = nc.dram_tensor("bq", [C], F32, kind="ExternalInput").ap()
    wk_h = nc.dram_tensor("wk", [512, C], F32, kind="ExternalInput").ap()
    bk_h = nc.dram_tensor("bk", [512], F32, kind="ExternalInput").ap()
    wv_h = nc.dram_tensor("wv", [512, C], F32, kind="ExternalInput").ap()
    bv_h = nc.dram_tensor("bv", [512], F32, kind="ExternalInput").ap()
    wo_h = nc.dram_tensor("wo", [C, C], F32, kind="ExternalInput").ap()
    bo_h = nc.dram_tensor("bo", [C], F32, kind="ExternalInput").ap()
    gw_h = nc.dram_tensor("gate_w", [NE, C], F32, kind="ExternalInput").ap()
    guw_h = nc.dram_tensor("gu_w", [NE, NI, C], F32, kind="ExternalInput").ap()
    dpw_h = nc.dram_tensor("dp_w", [NE, C, 512], F32, kind="ExternalInput").ap()
    shg_h = nc.dram_tensor("sh_gate_w", [SI, C], F32, kind="ExternalInput").ap()
    shu_h = nc.dram_tensor("sh_up_w", [SI, C], F32, kind="ExternalInput").ap()
    shd_h = nc.dram_tensor("sh_down_w", [C, SI], F32, kind="ExternalInput").ap()
    shg1_h = nc.dram_tensor("sh_gate1_w", [1, C], F32, kind="ExternalInput").ap()
    out_h = nc.dram_tensor("out", [T, C], F32, kind="ExternalOutput").ap()

    attn_dram = nc.dram_tensor("attn_dram", [128, NC * T], BF16, kind="Internal").ap()
    xn_dram = nc.dram_tensor("xn_dram", [128, NC * T], BF16, kind="Internal").ap()

    K = {k: nc.inline_tensor(v, "k_" + k).ap() for k, v in CST.items()}

    dbg = {}

    def dbg_dump(p, tag, src_ap, shape, dt):
        if tag not in debug:
            return
        hnd = nc.dram_tensor("dbg_" + tag, list(shape), dt, kind="ExternalOutput").ap()
        dbg[tag] = hnd
        p.op("sync", "dma_start", out=hnd, in_=src_ap, inc=sto, inck=16)

    est = contextlib.ExitStack()
    with est:
        ldx = Ctr(est, nc, "ldx")
        ldw = Ctr(est, nc, "ldw")
        pe = Ctr(est, nc, "pe")
        dve = Ctr(est, nc, "dve")
        act = Ctr(est, nc, "act")
        sto = Ctr(est, nc, "sto")

        SB = lambda name, shape, dt=F32: est.enter_context(nc.sbuf_tensor(name, list(shape), dt))

        cq_sb = SB("cq", (128, NH))
        ssq_sb = SB("ssq", (128, NH))
        ck_sb = SB("ck", (128, KV))
        ssk_sb = SB("ssk", (128, KV))
        idb_sb = SB("idb", (128, 128), BF16)
        idf_sb = SB("idf", (128, 128))
        ones_row_sb = SB("ones_row", (1, 128))
        ones_one_sb = SB("ones_one", (1, 1))
        onecol_sb = SB("onecol", (128, 1))
        onesb_col_sb = SB("onesb_col", (128, 1), BF16)
        bq_sb = SB("s_bq", (128, NH))
        bk_sb = SB("s_bk", (128, KV))
        bo_sb = SB("s_bo", (128, NC))
        bvb_sb = SB("bvb", (128, 512))
        g1T_sb = SB("g1T", (128, NC), BF16)
        WT_sb = SB("s_WT", (64, T))
        mean_sb = SB("s_mean", (128, NT))
        negmean_sb = SB("s_negmean", (128, NT))
        rstd_sb = SB("s_rstd", (128, NT))
        sa_scope = contextlib.ExitStack()
        gwT_sb = sa_scope.enter_context(nc.sbuf_tensor("gwT", [128, NC * 64], F32))
        Sb_sb = sa_scope.enter_context(nc.sbuf_tensor("Sb", [128, NE], F32))
        xn_scope = contextlib.ExitStack()
        xnT1_sb = xn_scope.enter_context(nc.sbuf_tensor("xnT1", [128, NC * TH], BF16, side="right"))
        xnT2_sb = xn_scope.enter_context(nc.sbuf_tensor("xnT2", [128, NC * TH], BF16, side="right"))
        xnT = (xnT1_sb, xnT2_sb)

        with nc.Block() as blk:
            p = Prog(blk)

            with (nc.psum_tensor("ps_s1", [128, 512], F32) as ps_s1,
                  nc.psum_tensor("ps_s2", [128, 64], F32) as ps_s2,
                  nc.sbuf_tensor("gw_nat", [64, C], F32) as gw_nat,
                  nc.sbuf_tensor("row_a", [1, 2048], F32) as row_a,
                  nc.sbuf_tensor("scol", [64, 1], F32) as scol,
                  nc.sbuf_tensor("srow", [1, 64], F32) as srow):

                for name, dst in (("rope_cq", cq_sb), ("rope_ssq", ssq_sb),
                                  ("rope_ck", ck_sb), ("rope_ssk", ssk_sb),
                                  ("id_bf", idb_sb), ("id_f32", idf_sb),
                                  ("ones_f32_row", ones_row_sb), ("ones_f32_one", ones_one_sb)):
                    p.op("sync", "dma_start", out=dst[:], in_=K[name], inc=ldx, inck=16)
                p.op("sync", "dma_start", out=gw_nat[:], in_=gw_h, inc=ldx, inck=16)
                p.op("vector", "memset", onesb_col_sb[:], 1.0)
                p.op("vector", "memset", onecol_sb[:], 1.0)

                def row_to_cols(dst_sb, n, dst128=True):
                    for i in range(n):
                        p.op("tensor", "matmul", out=ps_s2[:, i:i + 1],
                             lhsT=row_a[0:1, i * 128:(i + 1) * 128], rhs=ones_one_sb[:],
                             start=True, stop=True, inc=(pe if i == n - 1 else None))
                    p.wait("vector", pe)
                    p.op("vector", "tensor_copy", out=dst_sb[:, 0:n], in_=ps_s2[:, 0:n], inc=dve)
                    p.wait("tensor", dve)

                p.wait("tensor", ldx)
                for ct in range(NC):
                    p.op("tensor", "transpose", out=ps_s1[:, (ct % 8) * 64:(ct % 8) * 64 + 64],
                         in_=gw_nat[0:64, ct * 128:(ct + 1) * 128], identity=idf_sb[0:64, 0:64],
                         inc=(pe if ct % 8 == 7 else None))
                    if ct % 8 == 7:
                        p.wait("vector", pe)
                        p.op("vector", "tensor_copy",
                             out=gwT_sb[:, (ct - 7) * 64:(ct + 1) * 64],
                             in_=ps_s1[:, 0:512], inc=dve)
                        p.wait("tensor", dve)
                p.wait("vector", ldx)
                p.op("vector", "reduce_sum", out=scol[:], in_=gw_nat[:], axis=AX.X, inc=dve)
                p.wait("tensor", dve)
                p.op("tensor", "transpose", out=ps_s2[0:1, 0:64], in_=scol[:],
                     identity=idf_sb[0:64, 0:64], inc=pe)
                p.wait("vector", pe)
                p.op("vector", "tensor_copy", out=srow[:], in_=ps_s2[0:1, 0:64], inc=dve)
                p.wait("tensor", dve)
                p.op("tensor", "matmul", out=ps_s2[:, 0:64], lhsT=ones_row_sb[:],
                     rhs=srow[:], start=True, stop=True, inc=pe)
                p.wait("vector", pe)
                p.op("vector", "tensor_copy", out=Sb_sb[:], in_=ps_s2[:, 0:64], inc=dve)
                p.wait("tensor", dve)

                for src, dst_sb, n in ((bq_h[None, :], bq_sb, NH), (bk_h[None, :], bk_sb, KV),
                                       (bo_h[None, :], bo_sb, NC), (shg1_h, g1T_sb, NC)):
                    p.wait("sync", pe)
                    p.op("sync", "dma_start", out=row_a[0:1, 0:n * 128], in_=src, inc=ldx, inck=16)
                    p.wait("tensor", ldx)
                    row_to_cols(dst_sb, n)
                p.wait("sync", pe)
                p.op("sync", "dma_start", out=row_a[0:1, 0:512], in_=bv_h[None, :], inc=ldx, inck=16)
                p.wait("tensor", ldx)
                p.op("tensor", "matmul", out=ps_s1[:, 0:512], lhsT=ones_row_sb[:],
                     rhs=row_a[0:1, 0:512], start=True, stop=True, inc=pe)
                p.wait("vector", pe)
                p.op("vector", "tensor_copy", out=bvb_sb[:], in_=ps_s1[:, 0:512], inc=dve)
                p.wait("tensor", dve)
                p.emit()

            if "A" in phases:
                with contextlib.ExitStack() as ph:
                    PH_PS = lambda n, s, d=F32: ph.enter_context(nc.psum_tensor(n, list(s), d))
                    PH_SB = lambda n, s, d=F32: ph.enter_context(nc.sbuf_tensor(n, list(s), d))
                    psA_t = PH_PS("psA_t", (128, 1024), BF16)
                    psA_f = PH_PS("psA_f", (128, 512))
                    psA_r = PH_PS("psA_r", (64, 128))
                    psA_w = PH_PS("psA_w", (128, 64))
                    psA_w2 = PH_PS("psA_w2", (64, 128))
                    xt0 = PH_SB("xt0", (128, C))
                    xt1 = PH_SB("xt1", (128, C))
                    sq = PH_SB("sq", (128, C))
                    xn_bf = PH_SB("xn_bf", (128, C), BF16)
                    xT = PH_SB("xT", (128, NC * 128))
                    raw_all = PH_SB("raw_all", (64, NT * 128))
                    stat = PH_SB("stat", (128, 16))
                    l64 = PH_SB("l64", (128, 64))
                    s64 = PH_SB("s64", (128, 64))
                    e8 = PH_SB("e8", (128, 8))
                    logits_all = PH_SB("logits_all", (128, NT * 64))
                    rawT_all = PH_SB("rawT_all", (128, NT * 64))
                    top8_all = PH_SB("top8_all", (128, NT * 8))
                    top8 = PH_SB("top8", (128, 8))
                    xts = (xt0, xt1)
                    m_pe_f32T = [0] * NT
                    m_pe_raw = [0] * NT
                    m_act_xn = [0] * NT
                    m_dve_stats = [0] * NT

                    for tt in range(NT):
                        xt = xts[tt % 2]
                        so = 8 * (tt % 2)
                        if tt >= 2:
                            p.wait("sync", pe, m_pe_f32T[tt - 2])
                            p.wait("sync", act, m_act_xn[tt - 2])
                        ld = p.op("sync", "dma_start", out=xt[:],
                                  in_=x_h[tt * 128:(tt + 1) * 128, :], inc=ldx, inck=16)
                        p.wait("scalar", ldx, ld)
                        if tt >= 1:
                            p.wait("scalar", dve, m_dve_stats[tt - 1])
                        a_sq = p.op("scalar", "activation", out=sq[:], in_=xt[:],
                                    func=AF.Square, inc=act)
                        p.wait("vector", ldx, ld)
                        if tt >= 2:
                            p.wait("vector", act, m_act_xn[tt - 2] - 1)
                        p.op("vector", "reduce_sum", out=stat[:, so + 0:so + 1], in_=xt[:], axis=AX.X)
                        p.wait("vector", act, a_sq)
                        p.op("vector", "reduce_sum", out=stat[:, so + 1:so + 2], in_=sq[:], axis=AX.X)
                        p.op("vector", "tensor_scalar", out=mean_sb[:, tt:tt + 1],
                             in0=stat[:, so + 0:so + 1], scalar1=1.0 / C, scalar2=None, op0=ALU.mult)
                        p.op("vector", "tensor_scalar", out=negmean_sb[:, tt:tt + 1],
                             in0=stat[:, so + 0:so + 1], scalar1=-1.0 / C, scalar2=None, op0=ALU.mult)
                        p.op("vector", "tensor_scalar", out=stat[:, so + 2:so + 3],
                             in0=stat[:, so + 1:so + 2], scalar1=1.0 / C, scalar2=EPS,
                             op0=ALU.mult, op1=ALU.add)
                        d_nv = p.op("vector", "scalar_tensor_tensor", out=stat[:, so + 3:so + 4],
                                    in0=mean_sb[:, tt:tt + 1], scalar=mean_sb[:, tt:tt + 1],
                                    in1=stat[:, so + 2:so + 3], op0=ALU.mult, op1=ALU.subtract,
                                    inc=dve)
                        m_dve_stats[tt] = d_nv
                        p.wait("scalar", dve, d_nv)
                        a_sd = p.op("scalar", "activation", out=stat[:, so + 4:so + 5],
                                    in_=stat[:, so + 3:so + 4], func=AF.Sqrt, scale=-1.0,
                                    inc=act)
                        p.wait("vector", act, a_sd)
                        p.op("vector", "reciprocal", out=rstd_sb[:, tt:tt + 1], in_=stat[:, so + 4:so + 5])
                        d_nb = p.op("vector", "tensor_scalar", out=stat[:, so + 5:so + 6],
                                    in0=rstd_sb[:, tt:tt + 1], scalar1=negmean_sb[:, tt:tt + 1],
                                    scalar2=None, op0=ALU.mult, inc=dve)
                        p.wait("scalar", dve, d_nb)
                        if tt >= 1:
                            p.wait("scalar", pe, m_pe_f32T[tt - 1])
                        a_xn = p.op("scalar", "activation", out=xn_bf[:], in_=xt[:], func=AF.Identity,
                                    bias=stat[:, so + 5:so + 6], scale=rstd_sb[:, tt:tt + 1], inc=act)
                        m_act_xn[tt] = a_xn
                        half = tt // (NT // 2)
                        trel = tt - half * (NT // 2)
                        p.wait("tensor", act, a_xn)
                        dstv = xnT[half][:].rearrange("p (ct t) -> p ct t", t=TH)
                        for b in range(2):
                            p.wait("tensor", dve)
                            for k in range(8):
                                ct = b * 8 + k
                                p.op("tensor", "transpose", out=psA_t[:, k * 128:(k + 1) * 128],
                                     in_=xn_bf[:, ct * 128:(ct + 1) * 128], identity=idb_sb[:],
                                     inc=(pe if k == 7 else None))
                            p.wait("vector", pe)
                            p.op("vector", "tensor_copy",
                                 out=dstv[:, b * 8:(b + 1) * 8, trel * 128:(trel + 1) * 128],
                                 in_=psA_t[:].rearrange("p (k t) -> p k t", t=128), inc=dve)
                        if tt >= 1:
                            p.wait("vector", pe, m_pe_raw[tt - 1])
                        for b in range(4):
                            p.wait("tensor", dve)
                            for k in range(4):
                                ct = b * 4 + k
                                p.op("tensor", "transpose", out=psA_f[:, k * 128:(k + 1) * 128],
                                     in_=xt[:, ct * 128:(ct + 1) * 128], identity=idf_sb[:],
                                     inc=(pe if k == 3 else None))
                            if b == 3:
                                m_pe_f32T[tt] = pe.n
                            p.wait("vector", pe)
                            p.op("vector", "tensor_copy", out=xT[:, b * 512:(b + 1) * 512],
                                 in_=psA_f[:], inc=dve)
                        p.wait("tensor", dve)
                        for ct in range(NC):
                            p.op("tensor", "matmul", out=psA_r[:],
                                 lhsT=gwT_sb[:, ct * 64:(ct + 1) * 64],
                                 rhs=xT[:, ct * 128:(ct + 1) * 128],
                                 start=(ct == 0), stop=(ct == NC - 1),
                                 inc=(pe if ct == NC - 1 else None))
                        m_pe_raw[tt] = pe.n
                        p.wait("vector", pe, m_pe_raw[tt])
                        p.op("vector", "tensor_copy", out=raw_all[:, tt * 128:(tt + 1) * 128],
                             in_=psA_r[:], inc=dve)
                        p.wait("tensor", dve)

                    SCL = PH_SB("scl", (128, 4 * NT))
                    el_all = PH_SB("el_all", (128, NT * 64))
                    e8_all = PH_SB("e8_all", (128, NT * 8))
                    g64 = PH_SB("g64", (128, NT * 64))
                    wtile = PH_SB("wtile", (128, 2 * 64))
                    p.wait("tensor", dve)
                    for tt in range(NT):
                        pt = p.op("tensor", "transpose", out=psA_w[:],
                                  in_=raw_all[0:64, tt * 128:(tt + 1) * 128],
                                  identity=idf_sb[0:64, 0:64], inc=pe)
                        p.wait("vector", pe, pt)
                        p.op("vector", "scalar_tensor_tensor", out=s64[:], in0=Sb_sb[:],
                             scalar=negmean_sb[:, tt:tt + 1], in1=psA_w[:],
                             op0=ALU.mult, op1=ALU.add, inc=dve)
                        p.wait("tensor", dve)
                        p.op("vector", "tensor_scalar", out=logits_all[:, tt * 64:(tt + 1) * 64],
                             in0=s64[:], scalar1=rstd_sb[:, tt:tt + 1], scalar2=None, op0=ALU.mult)
                    for tt in range(NT):
                        p.op("vector", "max", out=top8_all[:, tt * 8:(tt + 1) * 8],
                             in_=logits_all[:, tt * 64:(tt + 1) * 64])
                    t8v = top8_all[:].rearrange("p (t k) -> p t k", k=8)
                    sclv = SCL[:].rearrange("p (g t) -> p g t", t=NT)
                    p.op("vector", "tensor_scalar", out=sclv[:, 0:1, :].rearrange("p g t -> p (g t)"),
                         in0=t8v[:, :, 0:1].rearrange("p t k -> p (t k)"),
                         scalar1=-1.0, scalar2=None, op0=ALU.mult)
                    d_p2 = p.op("vector", "tensor_copy",
                                out=sclv[:, 1:2, :].rearrange("p g t -> p (g t)"),
                                in_=t8v[:, :, 7:8].rearrange("p t k -> p (t k)"), inc=dve)
                    p.wait("scalar", dve, d_p2)
                    for tt in range(NT):
                        p.op("scalar", "activation", out=e8_all[:, tt * 8:(tt + 1) * 8],
                             in_=top8_all[:, tt * 8:(tt + 1) * 8], func=AF.Exp,
                             bias=SCL[:, tt:tt + 1])
                        a_el = p.op("scalar", "activation", out=el_all[:, tt * 64:(tt + 1) * 64],
                                    in_=logits_all[:, tt * 64:(tt + 1) * 64], func=AF.Exp,
                                    bias=SCL[:, tt:tt + 1], inc=act)
                    p.wait("vector", act, act.n)
                    for tt in range(NT):
                        p.op("vector", "reduce_sum", out=SCL[:, 2 * NT + tt:2 * NT + tt + 1],
                             in_=e8_all[:, tt * 8:(tt + 1) * 8], axis=AX.X)
                    for tt in range(NT):
                        p.op("vector", "tensor_single_scalar", out=g64[:, tt * 64:(tt + 1) * 64],
                             in_=logits_all[:, tt * 64:(tt + 1) * 64],
                             scalar=SCL[:, NT + tt:NT + tt + 1], op=ALU.is_ge)
                    p.op("vector", "reciprocal", out=SCL[:, 3 * NT:4 * NT], in_=SCL[:, 2 * NT:3 * NT])
                    for tt in range(NT):
                        p.op("vector", "tensor_tensor", out=el_all[:, tt * 64:(tt + 1) * 64],
                             in0=el_all[:, tt * 64:(tt + 1) * 64],
                             in1=g64[:, tt * 64:(tt + 1) * 64], op=ALU.mult)
                    m_pe_wT = [0] * NT
                    for tt in range(NT):
                        wt_buf = wtile[:, (tt % 2) * 64:(tt % 2) * 64 + 64]
                        if tt >= 2:
                            p.wait("vector", pe, m_pe_wT[tt - 2])
                        d_w = p.op("vector", "tensor_scalar", out=wt_buf,
                                   in0=el_all[:, tt * 64:(tt + 1) * 64],
                                   scalar1=SCL[:, 3 * NT + tt:3 * NT + tt + 1], scalar2=None,
                                   op0=ALU.mult, inc=dve)
                        p.wait("tensor", dve, d_w)
                        m_pe_wT[tt] = p.op("tensor", "transpose", out=psA_w2[:], in_=wt_buf,
                                           identity=idf_sb[:], inc=pe)
                        p.wait("vector", pe, m_pe_wT[tt])
                        p.op("vector", "tensor_copy", out=WT_sb[:, tt * 128:(tt + 1) * 128],
                             in_=psA_w2[:], inc=dve)
                        p.wait("tensor", dve)

                    if debug:
                        p.wait("sync", dve)
                    dbg_dump(p, "xnT1", xnT1_sb[:], (128, NC * TH), BF16)
                    dbg_dump(p, "xnT2", xnT2_sb[:], (128, NC * TH), BF16)
                    dbg_dump(p, "WT", WT_sb[:], (64, T), F32)
                    dbg_dump(p, "raw", raw_all[:], (64, NT * 128), F32)
                    dbg_dump(p, "logits", logits_all[:], (128, NT * 64), F32)
                    dbg_dump(p, "rawT", rawT_all[:], (128, NT * 64), F32)
                    dbg_dump(p, "top8", top8_all[:], (128, NT * 8), F32)
                    dbg_dump(p, "Sb", Sb_sb[:], (128, NE), F32)
                    dbg_dump(p, "mean", mean_sb[:], (128, NT), F32)
                    dbg_dump(p, "rstd", rstd_sb[:], (128, NT), F32)
                    p.emit()

            sa_scope.close()

            bcd = contextlib.ExitStack()
            if "B" in phases:
                BS = lambda n, s, d=F32: bcd.enter_context(nc.sbuf_tensor(n, list(s), d))
                kT_sb = BS("kT", (128, KV * T), BF16)
                v_sb = BS("v", (128, NT * 512), BF16)
                with contextlib.ExitStack() as ph:
                    psB_t = ph.enter_context(nc.psum_tensor("psB_t", [128, 1024], BF16))
                    psB_k0 = ph.enter_context(nc.psum_tensor("psB_k0", [128, 512], F32))
                    psB_k1 = ph.enter_context(nc.psum_tensor("psB_k1", [128, 512], F32))
                    wn_sb = ph.enter_context(nc.sbuf_tensor("wn", [128, 4 * C], BF16))
                    wkT_sb = ph.enter_context(nc.sbuf_tensor("wkT", [128, NC * 512], BF16))
                    wvT_sb = ph.enter_context(nc.sbuf_tensor("wvT", [128, NC * 512], BF16))
                    kf_sb = ph.enter_context(nc.sbuf_tensor("kf", [128, 2 * 512], F32))
                    tmp_sb = ph.enter_context(nc.sbuf_tensor("tmpB", [128, 512], F32))
                    psk = (psB_k0, psB_k1)

                    def load_transpose_w(src_2d, dst_sb, nrow_tiles):
                        nrt = nrow_tiles
                        p.wait("gpsimd", pe)
                        for gi in range(nrt):
                            ldv = p.op("gpsimd", "dma_start",
                                       out=wn_sb[:, gi * C:(gi + 1) * C],
                                       in_=src_2d[gi * 128:(gi + 1) * 128, :],
                                       inc=ldw, inck=16)
                        p.wait("tensor", ldw, ldv)
                        dstv = dst_sb[:].rearrange("p (ct r) -> p ct r", r=nrt * 128)
                        for gi in range(nrt):
                            for cb in range(2):
                                p.wait("tensor", dve)
                                for k in range(8):
                                    ct = cb * 8 + k
                                    p.op("tensor", "transpose",
                                         out=psB_t[:, k * 128:(k + 1) * 128],
                                         in_=wn_sb[:, gi * C + ct * 128:gi * C + (ct + 1) * 128],
                                         identity=idb_sb[:], inc=(pe if k == 7 else None))
                                p.wait("vector", pe)
                                p.op("vector", "tensor_copy",
                                     out=dstv[:, cb * 8:(cb + 1) * 8, gi * 128:(gi + 1) * 128],
                                     in_=psB_t[:].rearrange("p (k t) -> p k t", t=128), inc=dve)

                    load_transpose_w(wk_h, wkT_sb, 4)
                    m_wkT = dve.n
                    load_transpose_w(wv_h, wvT_sb, 4)
                    m_wvT = dve.n

                    p.wait("tensor", dve, m_wkT)
                    m_rope = {}
                    for gi in range(KV):
                        for ncg in range(4):
                            t0 = ncg * 512
                            gidx = gi * 4 + ncg
                            if gidx >= 2:
                                p.wait("tensor", act, m_rope.get(("a", gidx - 2), 0))
                            for ct in range(NC):
                                p.op("tensor", "matmul", out=psk[gidx % 2][:],
                                     lhsT=wkT_sb[:, ct * 512 + gi * 128:ct * 512 + (gi + 1) * 128],
                                     rhs=xnT[t0 // TH][:, ct * TH + (t0 % TH):ct * TH + (t0 % TH) + 512],
                                     start=(ct == 0), stop=(ct == NC - 1),
                                     inc=(pe if ct == NC - 1 else None))
                            mk = pe.n
                            p.wait("scalar", pe, mk)
                            if gidx >= 2:
                                p.wait("scalar", dve, m_rope.get(("d", gidx - 2), 0))
                            kf = kf_sb[:, (gidx % 2) * 512:(gidx % 2) * 512 + 512]
                            m_rope[("a", gidx)] = p.op("scalar", "activation", out=kf,
                                                       in_=psk[gidx % 2][:], func=AF.Identity,
                                                       bias=bk_sb[:, gi:gi + 1], inc=act)
                            p.wait("vector", act, m_rope[("a", gidx)])
                            p.op("vector", "tensor_copy", out=tmp_sb[0:64, :], in_=kf.tensor[64:128, (gidx % 2) * 512:(gidx % 2) * 512 + 512])
                            p.op("vector", "tensor_copy", out=tmp_sb[64:128, :], in_=kf.tensor[0:64, (gidx % 2) * 512:(gidx % 2) * 512 + 512])
                            p.op("vector", "tensor_scalar", out=kf, in0=kf,
                                 scalar1=ck_sb[:, gi:gi + 1], scalar2=None, op0=ALU.mult)
                            m_rope[("d", gidx)] = p.op("vector", "scalar_tensor_tensor",
                                                       out=kT_sb[:, gi * T + t0:gi * T + t0 + 512],
                                                       in0=tmp_sb[:], scalar=ssk_sb[:, gi:gi + 1],
                                                       in1=kf, op0=ALU.mult, op1=ALU.add, inc=dve)

                    p.wait("tensor", dve, m_wvT)
                    p.wait("tensor", act, m_rope[("a", 15)])
                    m_vev = {}
                    for tt in range(NT):
                        if tt >= 2:
                            p.wait("tensor", dve, m_vev.get(tt - 2, 0))
                        half = tt // (NT // 2)
                        trel = tt - half * (NT // 2)
                        for ct in range(NC):
                            p.op("tensor", "matmul", out=psk[tt % 2][:],
                                 lhsT=xnT[half][:, ct * TH + trel * 128:ct * TH + (trel + 1) * 128],
                                 rhs=wvT_sb[:, ct * 512:(ct + 1) * 512],
                                 start=(ct == 0), stop=(ct == NC - 1),
                                 inc=(pe if ct == NC - 1 else None))
                            mv = pe.n
                        p.wait("vector", pe, mv)
                        m_vev[tt] = p.op("vector", "tensor_tensor", out=v_sb[:, tt * 512:(tt + 1) * 512],
                                         in0=psk[tt % 2][:], in1=bvb_sb[:], op=ALU.add, inc=dve)

                    if debug:
                        p.wait("sync", dve)
                    dbg_dump(p, "kT", kT_sb[:], (128, KV * T), BF16)
                    dbg_dump(p, "v", v_sb[:], (128, NT * 512), BF16)
                    p.emit()

            if "C" in phases:
                qT_all = bcd.enter_context(nc.sbuf_tensor("qT_all", [128, NH * T], BF16))
                with contextlib.ExitStack() as ph:
                    PH_PS = lambda n, s, d=F32: ph.enter_context(nc.psum_tensor(n, list(s), d))
                    PH_SB = lambda n, s, d=F32: ph.enter_context(nc.sbuf_tensor(n, list(s), d))
                    psC_t = PH_PS("psC_t", (128, 1024), BF16)
                    psC_s0 = PH_PS("psC_s0", (128, 512))
                    psC_s1 = PH_PS("psC_s1", (128, 512))
                    wqn_sb = PH_SB("wqn", (128, 2 * C), BF16)
                    wqT_sb = PH_SB("wqT", (128, 2 * C), BF16)
                    qf_sb = PH_SB("qf", (128, 2 * 512))
                    qtmp_sb = PH_SB("qtmp", (128, 512))
                    pss = (psC_s0, psC_s1)
                    m_wq_pe = [0] * NH
                    m_q_pe = [0] * NH
                    m_aq = 0
                    for h in range(NH):
                        par = h % 2
                        if h >= 2:
                            p.wait("gpsimd", pe, m_wq_pe[h - 2])
                        lq = p.op("gpsimd", "dma_start", out=wqn_sb[:, par * C:(par + 1) * C],
                                  in_=wq_h[h * 128:(h + 1) * 128, :], inc=ldw, inck=16)
                        p.wait("tensor", ldw, lq)
                        if h >= 2:
                            p.wait("tensor", pe, m_q_pe[h - 2])
                        wqT = wqT_sb[:, par * C:(par + 1) * C]
                        for cb in range(2):
                            p.wait("tensor", dve)
                            for k in range(8):
                                ct = cb * 8 + k
                                p.op("tensor", "transpose", out=psC_t[:, k * 128:(k + 1) * 128],
                                     in_=wqn_sb[:, par * C + ct * 128:par * C + (ct + 1) * 128],
                                     identity=idb_sb[:], inc=(pe if k == 7 else None))
                            p.wait("vector", pe)
                            p.op("vector", "tensor_copy",
                                 out=wqT.rearrange("p (cb2 r) -> p cb2 r", r=128)[:, cb * 8:(cb + 1) * 8, :],
                                 in_=psC_t[:].rearrange("p (k t) -> p k t", t=128), inc=dve)
                        m_wq_pe[h] = pe.n
                        p.wait("tensor", dve)
                        for ncg in range(4):
                            qpar = ncg % 2
                            t0 = ncg * 512
                            if h + ncg > 0:
                                p.wait("tensor", dve)
                            for ct in range(NC):
                                p.op("tensor", "matmul", out=pss[qpar][:],
                                     lhsT=wqT[:, ct * 128:(ct + 1) * 128],
                                     rhs=xnT[t0 // TH][:, ct * TH + (t0 % TH):ct * TH + (t0 % TH) + 512],
                                     start=(ct == 0), stop=(ct == NC - 1),
                                     inc=(pe if ct == NC - 1 else None))
                            mq = pe.n
                            p.wait("scalar", pe, mq)
                            qf = qf_sb[:, qpar * 512:(qpar + 1) * 512]
                            m_aq = p.op("scalar", "activation", out=qf, in_=pss[qpar][:],
                                        func=AF.Identity, bias=bq_sb[:, h:h + 1], inc=act)
                            p.wait("vector", act, m_aq)
                            p.op("vector", "tensor_copy", out=qtmp_sb[0:64, :],
                                 in_=qf_sb[64:128, qpar * 512:(qpar + 1) * 512])
                            p.op("vector", "tensor_copy", out=qtmp_sb[64:128, :],
                                 in_=qf_sb[0:64, qpar * 512:(qpar + 1) * 512])
                            p.op("vector", "tensor_scalar", out=qf, in0=qf,
                                 scalar1=cq_sb[:, h:h + 1], scalar2=None, op0=ALU.mult)
                            p.op("vector", "scalar_tensor_tensor",
                                 out=qT_all[:, h * T + t0:h * T + t0 + 512],
                                 in0=qtmp_sb[:], scalar=ssq_sb[:, h:h + 1],
                                 in1=qf, op0=ALU.mult, op1=ALU.add, inc=dve)
                        m_q_pe[h] = pe.n
                p.wait("sync", dve)
                sp1 = p.op("sync", "dma_start", out=xn_dram[:, 0:NC * TH], in_=xnT1_sb[:],
                           inc=sto, inck=16)
                sp2 = p.op("sync", "dma_start", out=xn_dram[:, NC * TH:], in_=xnT2_sb[:],
                           inc=sto, inck=16)
                p.emit()
                xn_scope.close()

                ctxT_sb = bcd.enter_context(nc.sbuf_tensor("ctxT", [128, NH * T], BF16))
                with contextlib.ExitStack() as ph:
                    PH_PS = lambda n, s, d=F32: ph.enter_context(nc.psum_tensor(n, list(s), d))
                    PH_SB = lambda n, s, d=F32: ph.enter_context(nc.sbuf_tensor(n, list(s), d))
                    psC_s0 = PH_PS("psC2_s0", (128, 512))
                    psC_s1 = PH_PS("psC2_s1", (128, 512))
                    psC_o = PH_PS("psC_o", (128, 512))
                    psC_d = PH_PS("psC_d", (1, 512))
                    psC_b = PH_PS("psC_b", (128, 512))
                    maskd_sb = PH_SB("maskd", (128, 4 * 512))
                    exp_sb = PH_SB("expS", (128, NT * 512), BF16)
                    den_sb = PH_SB("den", (1, 2 * 512))
                    rden_sb = PH_SB("rden", (1, 2 * 512))
                    rdb_sb = PH_SB("rdb", (128, 512))
                    pss = (psC_s0, psC_s1)
                    lm = p.op("sync", "dma_start", out=maskd_sb[:], in_=K["maskdiag"],
                              inc=ldx, inck=16)
                    p.wait("vector", ldx, lm)
                    p.wait("vector", sto, sp2)
                    p.wait("tensor", dve)
                    exp_hist = []
                    m_av = {}
                    m_ctx = 0
                    m_ad = 0
                    m_dr = 0
                    for h in range(NH):
                        g = h // 4
                        for tc in range(4):
                            for st in range(NT):
                                spar = len(exp_hist) % 2
                                if len(exp_hist) >= 2:
                                    p.wait("tensor", act, exp_hist[-2])
                                elif h == 0 and tc == 0 and st == 0:
                                    p.wait("tensor", act, m_aq)
                                ms = p.op("tensor", "matmul", out=pss[spar][:],
                                          lhsT=kT_sb[:, g * T + st * 128:g * T + (st + 1) * 128],
                                          rhs=qT_all[:, h * T + tc * 512:h * T + (tc + 1) * 512],
                                          start=True, stop=True, inc=pe)
                                r = st - 4 * tc
                                if 0 <= r <= 3:
                                    p.wait("vector", pe, ms)
                                    dm = p.op("vector", "tensor_tensor", out=pss[spar][:],
                                              in0=pss[spar][:], in1=maskd_sb[:, r * 512:(r + 1) * 512],
                                              op=ALU.add, inc=dve)
                                    p.wait("scalar", dve, dm)
                                else:
                                    p.wait("scalar", pe, ms)
                                if (h, tc) != (0, 0) and st == 0:
                                    p.wait("scalar", pe, m_av[(h, tc - 1) if tc > 0 else (h - 1, 3)])
                                bias_ap = onecol_sb[:] if st < 4 * tc else 0.0
                                e_ms = p.op("scalar", "activation",
                                            out=exp_sb[:, st * 512:(st + 1) * 512],
                                            in_=pss[spar][:], func=AF.Exp,
                                            bias=bias_ap, inc=act)
                                exp_hist.append(e_ms)
                            p.wait("tensor", act, exp_hist[-1])
                            if (h, tc) != (0, 0):
                                p.wait("tensor", act, m_ad)
                                p.wait("tensor", dve, m_ctx)
                            for st in range(NT):
                                p.op("tensor", "matmul", out=psC_d[:],
                                     lhsT=onesb_col_sb[:], rhs=exp_sb[:, st * 512:(st + 1) * 512],
                                     start=(st == 0), stop=(st == NT - 1),
                                     inc=(pe if st == NT - 1 else None))
                            m_denv = pe.n
                            for st in range(NT):
                                p.op("tensor", "matmul", out=psC_o[:],
                                     lhsT=v_sb[:, st * 512 + g * 128:st * 512 + (g + 1) * 128],
                                     rhs=exp_sb[:, st * 512:(st + 1) * 512],
                                     start=(st == 0), stop=(st == NT - 1),
                                     inc=(pe if st == NT - 1 else None))
                            m_av[(h, tc)] = pe.n
                            dpar = (4 * h + tc) % 2
                            p.wait("scalar", pe, m_denv)
                            p.wait("scalar", dve, m_dr)
                            m_ad = p.op("scalar", "copy", out=den_sb[0:1, dpar * 512:(dpar + 1) * 512],
                                        in_=psC_d[:], inc=act)
                            p.wait("vector", act, m_ad)
                            m_dr = p.op("vector", "reciprocal",
                                        out=rden_sb[0:1, dpar * 512:(dpar + 1) * 512],
                                        in_=den_sb[0:1, dpar * 512:(dpar + 1) * 512], inc=dve)
                            p.wait("tensor", dve, m_dr)
                            pb = p.op("tensor", "matmul", out=psC_b[:], lhsT=ones_row_sb[:],
                                      rhs=rden_sb[0:1, dpar * 512:(dpar + 1) * 512],
                                      start=True, stop=True, inc=pe)
                            p.wait("vector", pe, pb)
                            p.op("vector", "tensor_copy", out=rdb_sb[:], in_=psC_b[:], inc=dve)
                            p.wait("vector", pe, m_av[(h, tc)])
                            m_ctx = p.op("vector", "tensor_tensor",
                                         out=ctxT_sb[:, h * T + tc * 512:h * T + (tc + 1) * 512],
                                         in0=psC_o[:], in1=rdb_sb[:], op=ALU.mult, inc=dve)
                    if debug:
                        p.wait("sync", dve)
                    dbg_dump(p, "ctxT", ctxT_sb[:], (128, NH * T), BF16)
                    p.emit()

            if "D" in phases:
                with contextlib.ExitStack() as ph:
                    psD_t = ph.enter_context(nc.psum_tensor("psD_t", [128, 1024], BF16))
                    psD_a = ph.enter_context(nc.psum_tensor("psD_a", [128, 512], F32))
                    psD_b = ph.enter_context(nc.psum_tensor("psD_b", [128, 512], F32))
                    won_sb = ph.enter_context(nc.sbuf_tensor("won", [128, 2 * C], BF16))
                    woT_sb = ph.enter_context(nc.sbuf_tensor("woT", [128, 2 * C], BF16))
                    ast_sb = ph.enter_context(nc.sbuf_tensor("ast", [128, 2 * T], BF16))
                    psd = (psD_a, psD_b)
                    p.wait("gpsimd", pe)
                    p.wait("gpsimd", dve)
                    m_wo_pe = [0] * NC
                    m_out_pe = [0] * NC
                    m_ast = [0] * NC
                    m_aev = {}
                    for jo in range(NC):
                        par = jo % 2
                        if jo >= 2:
                            p.wait("gpsimd", pe, m_wo_pe[jo - 2])
                        lw = p.op("gpsimd", "dma_start", out=won_sb[:, par * C:(par + 1) * C],
                                  in_=wo_h[jo * 128:(jo + 1) * 128, :], inc=ldw, inck=16)
                        p.wait("tensor", ldw, lw)
                        if jo >= 2:
                            p.wait("tensor", pe, m_out_pe[jo - 2])
                        woT = woT_sb[:, par * C:(par + 1) * C]
                        for cb in range(2):
                            p.wait("tensor", dve)
                            for k in range(8):
                                jt = cb * 8 + k
                                p.op("tensor", "transpose", out=psD_t[:, k * 128:(k + 1) * 128],
                                     in_=won_sb[:, par * C + jt * 128:par * C + (jt + 1) * 128],
                                     identity=idb_sb[:], inc=(pe if k == 7 else None))
                            p.wait("vector", pe)
                            p.op("vector", "tensor_copy",
                                 out=woT.rearrange("p (cb2 r) -> p cb2 r", r=128)[:, cb * 8:(cb + 1) * 8, :],
                                 in_=psD_t[:].rearrange("p (k t) -> p k t", t=128), inc=dve)
                        m_wo_pe[jo] = pe.n
                        p.wait("tensor", dve)
                        if jo >= 2:
                            p.wait("scalar", sto, m_ast[jo - 2])
                        for tc in range(4):
                            tpar = tc % 2
                            if jo + tc > 0:
                                p.wait("tensor", act, m_aev.get((jo, tc - 2),
                                                               m_aev.get((jo - 1, 2 + tc), 0)))
                            for jt in range(NC):
                                p.op("tensor", "matmul", out=psd[tpar][:],
                                     lhsT=woT[:, jt * 128:(jt + 1) * 128],
                                     rhs=ctxT_sb[:, jt * T + tc * 512:jt * T + (tc + 1) * 512],
                                     start=(jt == 0), stop=(jt == NC - 1),
                                     inc=(pe if jt == NC - 1 else None))
                            mo = pe.n
                            p.wait("scalar", pe, mo)
                            m_aev[(jo, tc)] = p.op("scalar", "activation",
                                                   out=ast_sb[:, par * T + tc * 512:par * T + (tc + 1) * 512],
                                                   in_=psd[tpar][:], func=AF.Identity,
                                                   bias=bo_sb[:, jo:jo + 1], inc=act)
                        m_out_pe[jo] = pe.n
                        p.wait("sync", act, m_aev[(jo, 3)])
                        m_ast[jo] = p.op("sync", "dma_start",
                                         out=attn_dram[:, jo * T:(jo + 1) * T],
                                         in_=ast_sb[:, par * T:(par + 1) * T], inc=sto, inck=16)
                    p.wait("sync", sto)
                    dbg_dump(p, "attn", attn_dram, (128, NC * T), BF16)
                    p.emit()

            if "EF" in phases:
                bcd.close()
                guwT_dramA = nc.dram_tensor("guwT_dramA", [32 * 8 * 128, C], BF16,
                                            kind="Internal").ap()
                guwT_dramB = nc.dram_tensor("guwT_dramB", [33 * 8 * 128, C], BF16,
                                            kind="Internal").ap()
                dpwT_dram = nc.dram_tensor("dpwT_dram", [(NE + 1) * 128, 4 * C], BF16,
                                           kind="Internal").ap()

                def guwT_dram_slice(eidx, si):
                    if eidx < 32:
                        r = (eidx * 8 + si) * 128
                        return guwT_dramA[r:r + 128, :]
                    r = ((eidx - 32) * 8 + si) * 128
                    return guwT_dramB[r:r + 128, :]
                ef = contextlib.ExitStack()
                EFS = lambda n, s, d=F32: ef.enter_context(nc.sbuf_tensor(n, list(s), d))
                xnh_sb = EFS("xnh", (128, NC * TH), BF16)
                acc_sb = EFS("acc", (128, NC * TH))
                p.wait("gpsimd", pe)
                p.wait("gpsimd", dve)
                p.wait("gpsimd", act)
                p.wait("sync", pe)
                p.wait("sync", dve)
                for hf in range(2):
                    with contextlib.ExitStack() as eph:
                        PS2 = lambda n, s, d=F32: eph.enter_context(nc.psum_tensor(n + str(hf), list(s), d))
                        SB2 = lambda n, s, d=F32: eph.enter_context(nc.sbuf_tensor(n + str(hf), list(s), d))
                        ps_g = PS2("ps_g", (128, 1024))
                        ps_u = PS2("ps_u", (128, 1024))
                        ps_d0 = PS2("ps_d0", (128, 512))
                        ps_d1 = PS2("ps_d1", (128, 512))
                        ps_t0 = PS2("ps_t0", (128, 1024), BF16)
                        ps_t1 = PS2("ps_t1", (128, 1024), BF16)
                        if hf == 0:
                            gu_nat = SB2("gu_nat", (128, 2 * C), BF16)
                            guwT = SB2("guwT", (128, 2 * C), BF16)
                            dp_nat = SB2("dp_nat", (128, NC * 512), BF16)
                        else:
                            guwT4 = SB2("guwT4", (128, 4 * C), BF16)
                        dpwT = SB2("dpwT", (128, 4 * C), BF16)
                        interT = SB2("interT", (128, 4 * TH), BF16)
                        sg_sb = SB2("sg", (128, 1024))
                        p1_sb = sg_sb
                        Wb_sb = SB2("Wb", (128, TH))
                        wrow_sb = SB2("wrow", (1, 2 * TH))
                        st_misc2 = None
                        shrow_sb = wrow_sb
                        psd = (ps_d0, ps_d1)
                        pst = (ps_t0, ps_t1)

                        if hf == 1:
                            p.wait("gpsimd", dve)
                            p.wait("gpsimd", pe)
                            p.wait("sync", pe)
                            p.wait("sync", dve)
                        rs = p.op("sync", "dma_start", out=xnh_sb[:],
                                  in_=xn_dram[:, hf * NC * TH:(hf + 1) * NC * TH],
                                  inc=ldx, inck=16)
                        for jo in range(NC):
                            p.op("gpsimd", "dma_start",
                                 out=acc_sb[:, jo * TH:(jo + 1) * TH],
                                 in_=attn_dram[:, jo * T + hf * TH:jo * T + hf * TH + TH],
                                 inc=ldw, inck=16)
                        m_acc_init = ldw.n
                        m_xnh = rs

                        st_dve = {"wb": 0, "intT": [0] * 4, "accl": 0}
                        st_pe = {"dp_tr": 0, "up": [0] * 8, "dn": 0,
                                 "wb": 0, "gu_done": [0] * 8}
                        tr_hist = []
                        spill_gu = [0] * 8
                        spill_dp = [0]
                        st_misc = {"dp_ldw": 0, "wrow_ld": 0}
                        acc_hist = []
                        order = (0, 4, 1, 5, 2, 6, 3, 7)

                        def tr_batch(srcs, dst_ap):
                            b = pst[len(tr_hist) % 2]
                            if len(tr_hist) >= 2:
                                p.wait("tensor", dve, tr_hist[-2])
                            for k, s_ap in enumerate(srcs):
                                p.op("tensor", "transpose", out=b[:, k * 128:(k + 1) * 128],
                                     in_=s_ap, identity=idb_sb[:],
                                     inc=(pe if k == 7 else None))
                            p.wait("vector", pe)
                            ev = p.op("vector", "tensor_copy", out=dst_ap,
                                      in_=b[:].rearrange("p (k t) -> p k t", t=128), inc=dve)
                            tr_hist.append(ev)
                            return ev

                        def up_pair_down(eidx, acquire_gu, acquire_dp, post_up=None,
                                         next_wb=None, gu_evac_ms=None, gu_lhsT=None):
                            if gu_lhsT is None:
                                gu_lhsT = lambda si, ct: guwT[:, (si % 2) * C + ct * 128:(si % 2) * C + (ct + 1) * 128]
                            """Shared core: acquire weights (callbacks), up matmuls in
                            pair order, silu/mul/Wb -> interT, down matmuls -> acc.
                            gu_evac_ms(si) -> (first_half_ms, second_half_ms) dve
                            milestones for the slice's transposed-weight halves
                            (None = gate on ldw only, handled by acquire_gu)."""
                            acquire_dp()
                            acquire_gu(0)
                            for si, s in enumerate(order):
                                par = si % 2
                                if si + 1 < 8:
                                    acquire_gu(si + 1)
                                ev = gu_evac_ms(si) if gu_evac_ms else (0, 0)
                                dst = ps_g if si % 2 == 0 else ps_u
                                j = s % 4
                                if si % 2 == 0:
                                    p.wait("tensor", act, act.n)
                                else:
                                    p.wait("tensor", dve, st_dve["intT"][j - 1] if j > 0 else st_dve["accl"])
                                if ev[1]:
                                    p.wait("tensor", dve, ev[1])
                                for tc in range(2):
                                    for ct in range(NC):
                                        p.op("tensor", "matmul",
                                             out=dst[:, tc * 512:(tc + 1) * 512],
                                             lhsT=gu_lhsT(si, ct),
                                             rhs=xnh_sb[:, ct * TH + tc * 512:ct * TH + (tc + 1) * 512],
                                             start=(ct == 0), stop=(ct == NC - 1),
                                             inc=(pe if ct == NC - 1 and tc == 1 else None))
                                st_pe["up"][si] = pe.n
                                if post_up is not None:
                                    post_up(si)
                                if si % 2 == 1:
                                    p.wait("scalar", pe, st_pe["up"][si - 1])
                                    a_sg = p.op("scalar", "activation", out=sg_sb[:],
                                                in_=ps_g[:], func=AF.Silu, inc=act)
                                    p.wait("vector", act, a_sg)
                                    p.wait("vector", pe, st_pe["up"][si])
                                    p.op("vector", "tensor_tensor", out=p1_sb[:],
                                         in0=sg_sb[:], in1=ps_u[:], op=ALU.mult)
                                    st_dve["intT"][j] = p.op(
                                        "vector", "tensor_tensor",
                                        out=interT[:, j * TH:(j + 1) * TH],
                                        in0=p1_sb[:], in1=Wb_sb[:],
                                        op=ALU.mult, inc=dve)
                            if next_wb is not None:
                                next_wb()
                            p.wait("tensor", dve, st_dve["intT"][3])
                            p.wait("tensor", ldw, st_misc["dp_ldw"])
                            for nacc in range(NC * 2):
                                ht, tc = nacc // 2, nacc % 2
                                dpar = nacc % 2
                                if nacc == 0:
                                    p.wait("tensor", dve, st_dve["wb"])
                                if len(acc_hist) >= 2:
                                    p.wait("tensor", dve, acc_hist[-2])
                                for j in range(4):
                                    p.op("tensor", "matmul", out=psd[dpar][:],
                                         lhsT=dpwT[:, j * C + ht * 128:j * C + (ht + 1) * 128],
                                         rhs=interT[:, j * TH + tc * 512:j * TH + (tc + 1) * 512],
                                         start=(j == 0), stop=(j == 3),
                                         inc=(pe if j == 3 else None))
                                mdn = pe.n
                                p.wait("vector", pe, mdn)
                                av = p.op("vector", "tensor_tensor",
                                          out=acc_sb[:, ht * TH + tc * 512:ht * TH + (tc + 1) * 512],
                                          in0=psd[dpar][:],
                                          in1=acc_sb[:, ht * TH + tc * 512:ht * TH + (tc + 1) * 512],
                                          op=ALU.add, inc=dve)
                                acc_hist.append(av)
                            st_pe["dn"] = pe.n
                            st_dve["accl"] = acc_hist[-1]

                        def expert_h0(gu_srcs, dp_src, eidx, next_wb):
                            lgu = [0] * 8
                            ev_ms = [[0, 0] for _ in range(8)]

                            def gu_load(si):
                                p.wait("gpsimd", pe,
                                       st_pe["gu_done"][si - 2 if si >= 2 else si + 6])
                                lgu[si] = p.op("gpsimd", "dma_start",
                                               out=gu_nat[:, (si % 2) * C:(si % 2 + 1) * C],
                                               in_=gu_srcs[si], inc=ldw, inck=16)
                            gu_load(0)
                            gu_load(1)

                            def acquire_dp():
                                p.wait("gpsimd", pe, st_pe["dp_tr"])
                                for ht in range(NC):
                                    ldp = p.op("gpsimd", "dma_start",
                                               out=dp_nat[:, ht * 512:(ht + 1) * 512],
                                               in_=dp_src[ht * 128:(ht + 1) * 128, :],
                                               inc=ldw, inck=16)
                                p.wait("tensor", ldw, ldp)
                                dpv = dpwT[:].rearrange("p (j h) -> p j h", j=4)
                                for hb in range(8):
                                    j, hh = hb % 4, (hb // 4) * 8
                                    if hb == 0:
                                        p.wait("vector", sto, spill_dp[0])
                                    tr_batch([dp_nat[:, (hh + k) * 512 + j * 128:(hh + k) * 512 + (j + 1) * 128]
                                              for k in range(8)],
                                             dpv[:, j, hh * 128:(hh + 8) * 128])
                                st_pe["dp_tr"] = pe.n
                                p.wait("sync", dve, tr_hist[-1])
                                spill_dp[0] = p.op("sync", "dma_start",
                                                   out=dpwT_dram[eidx * 128:(eidx + 1) * 128, :],
                                                   in_=dpwT[:], inc=sto, inck=16)

                            def acquire_gu(si):
                                par = si % 2
                                p.wait("tensor", ldw, lgu[si])
                                guv = guwT[:].rearrange("p (q ct r) -> p q ct r", q=2, r=128)
                                for cb in range(2):
                                    if cb == 0:
                                        p.wait("vector", sto, spill_gu[si])
                                    tr_batch([gu_nat[:, par * C + (cb * 8 + k) * 128:par * C + (cb * 8 + k + 1) * 128]
                                              for k in range(8)],
                                             guv[:, par, cb * 8:(cb + 1) * 8, :])
                                    ev_ms[si][cb] = tr_hist[-1]
                                st_pe["gu_done"][si] = pe.n
                                if si + 2 < 8:
                                    gu_load(si + 2)
                                p.wait("sync", dve, tr_hist[-1])
                                sgu = p.op("sync", "dma_start",
                                           out=guwT_dram_slice(eidx, si),
                                           in_=guwT[:, par * C:(par + 1) * C], inc=sto, inck=16)
                                spill_gu[(si + 2) % 8] = sgu

                            up_pair_down(eidx, acquire_gu, acquire_dp,
                                         next_wb=next_wb,
                                         gu_evac_ms=lambda si: ev_ms[si])

                        def expert_h1(eidx, next_wb):
                            lgu = [0] * 8

                            def gu_load(si):
                                p.wait("gpsimd", pe,
                                       st_pe["up"][si - 4 if si >= 4 else si + 4])
                                lgu[si] = p.op("gpsimd", "dma_start",
                                               out=guwT4[:, (si % 4) * C:(si % 4 + 1) * C],
                                               in_=guwT_dram_slice(eidx, si),
                                               inc=ldw, inck=16)
                            for si0 in range(4):
                                gu_load(si0)

                            def acquire_dp():
                                p.wait("gpsimd", pe, st_pe["dn"])
                                st_misc["dp_ldw"] = p.op(
                                    "gpsimd", "dma_start", out=dpwT[:],
                                    in_=dpwT_dram[eidx * 128:(eidx + 1) * 128, :],
                                    inc=ldw, inck=16)

                            def acquire_gu(si):
                                p.wait("tensor", ldw, lgu[si])

                            def post_up(si):
                                if si + 4 < 8:
                                    gu_load(si + 4)

                            up_pair_down(eidx, acquire_gu, acquire_dp, post_up,
                                         next_wb=next_wb,
                                         gu_lhsT=lambda si, ct: guwT4[:, (si % 4) * C + ct * 128:(si % 4) * C + (ct + 1) * 128])

                        def wrow_load(e):
                            wpar = e % 2
                            p.wait("sync", pe, st_pe["wb"])
                            st_misc["wrow_ld"] = p.op(
                                "sync", "dma_start",
                                out=wrow_sb[0:1, wpar * TH:(wpar + 1) * TH],
                                in_=WT_sb[e:e + 1, hf * TH:(hf + 1) * TH],
                                inc=ldx, inck=16)

                        def wb_expert(e):
                            def f():
                                wpar = e % 2
                                p.wait("tensor", ldx, st_misc["wrow_ld"])
                                if acc_hist:
                                    p.wait("tensor", dve, acc_hist[-1])
                                p.wait("tensor", dve, st_dve["wb"])
                                for tc in range(2):
                                    p.op("tensor", "matmul", out=psd[tc][:],
                                         lhsT=ones_row_sb[:],
                                         rhs=wrow_sb[0:1, wpar * TH + tc * 512:wpar * TH + (tc + 1) * 512],
                                         start=True, stop=True,
                                         inc=(pe if tc == 1 else None))
                                mb = pe.n
                                p.wait("vector", pe, mb)
                                for tc in range(2):
                                    st_dve["wb"] = p.op("vector", "tensor_copy",
                                                        out=Wb_sb[:, tc * 512:(tc + 1) * 512],
                                                        in_=psd[tc][:], inc=dve)
                                st_pe["wb"] = pe.n
                            return f

                        def wb_shared():
                            p.wait("tensor", ldx, m_xnh)
                            if acc_hist:
                                p.wait("tensor", dve, acc_hist[-1])
                            for tc in range(2):
                                p.wait("tensor", act, act.n)
                                p.wait("tensor", dve, st_dve["wb"])
                                for ct in range(NC):
                                    p.op("tensor", "matmul", out=psd[0][0:1, :],
                                         lhsT=g1T_sb[:, ct:ct + 1],
                                         rhs=xnh_sb[:, ct * TH + tc * 512:ct * TH + (tc + 1) * 512],
                                         start=(ct == 0), stop=(ct == NC - 1),
                                         inc=(pe if ct == NC - 1 else None))
                                mg = pe.n
                                p.wait("scalar", pe, mg)
                                ag = p.op("scalar", "activation",
                                          out=shrow_sb[0:1, TH + tc * 512:TH + (tc + 1) * 512],
                                          in_=psd[0][0:1, :], func=AF.Sigmoid, inc=act)
                                p.wait("tensor", act, ag)
                                mb = p.op("tensor", "matmul", out=psd[0][:],
                                          lhsT=ones_row_sb[:],
                                          rhs=shrow_sb[0:1, TH + tc * 512:TH + (tc + 1) * 512],
                                          start=True, stop=True, inc=pe)
                                p.wait("vector", pe, mb)
                                st_dve["wb"] = p.op("vector", "tensor_copy",
                                                    out=Wb_sb[:, tc * 512:(tc + 1) * 512],
                                                    in_=psd[0][:], inc=dve)
                            st_pe["wb"] = pe.n

                        p.wait("tensor", ldw, m_acc_init)
                        p.wait("tensor", ldx, m_xnh)
                        if hf == 0:
                            sh_gu = [shg_h[j * 128:(j + 1) * 128, :] if si % 2 == 0
                                     else shu_h[j * 128:(j + 1) * 128, :]
                                     for si, j in enumerate([s % 4 for s in order])]
                            wb_shared()
                            if n_experts:
                                wrow_load(0)
                            expert_h0(sh_gu, shd_h, NE,
                                      next_wb=(wb_expert(0) if n_experts else None))
                            for e in range(n_experts):
                                gu_srcs = [guw_h[e, s * 128:(s + 1) * 128, :] for s in order]
                                if e + 1 < n_experts:
                                    wrow_load(e + 1)
                                expert_h0(gu_srcs, dpw_h[e], e,
                                          next_wb=(wb_expert(e + 1) if e + 1 < n_experts else None))
                        else:
                            wb_shared()
                            if n_experts:
                                wrow_load(0)
                            expert_h1(NE, next_wb=(wb_expert(0) if n_experts else None))
                            for e in range(n_experts):
                                if e + 1 < n_experts:
                                    wrow_load(e + 1)
                                expert_h1(e, next_wb=(wb_expert(e + 1) if e + 1 < n_experts else None))

                        with contextlib.ExitStack() as fin:
                            xt_f = fin.enter_context(nc.sbuf_tensor("xt_f" + str(hf), [128, C], F32))
                            out_f = fin.enter_context(nc.sbuf_tensor("out_f" + str(hf), [128, C], F32))
                            psF = ps_g
                            m_fin_st = [0] * 8
                            m_fin_dve = [0] * 8
                            for tb in range(8):
                                if tb >= 1:
                                    p.wait("sync", dve, m_fin_dve[tb - 1])
                                lx = p.op("sync", "dma_start", out=xt_f[:],
                                          in_=x_h[hf * TH + tb * 128:hf * TH + (tb + 1) * 128, :],
                                          inc=ldx, inck=16)
                                for hb in range(2):
                                    if tb == 0 and hb == 0:
                                        p.wait("tensor", dve, acc_hist[-1])
                                        p.wait("tensor", act, act.n)
                                    else:
                                        p.wait("tensor", dve)
                                    for k in range(8):
                                        ht = hb * 8 + k
                                        p.op("tensor", "transpose",
                                             out=psF[:, k * 128:(k + 1) * 128],
                                             in_=acc_sb[:, ht * TH + tb * 128:ht * TH + (tb + 1) * 128],
                                             identity=idf_sb[:], inc=(pe if k == 7 else None))
                                    p.wait("vector", pe)
                                    p.wait("vector", ldx, lx)
                                    if tb >= 1 and hb == 0:
                                        p.wait("vector", sto, m_fin_st[tb - 1])
                                    m_fin_dve[tb] = p.op(
                                        "vector", "tensor_tensor",
                                        out=out_f[:, hb * 1024:(hb + 1) * 1024],
                                        in0=psF[:], in1=xt_f[:, hb * 1024:(hb + 1) * 1024],
                                        op=ALU.add, inc=dve)
                                p.wait("sync", dve, m_fin_dve[tb])
                                m_fin_st[tb] = p.op("sync", "dma_start",
                                                    out=out_h[hf * TH + tb * 128:hf * TH + (tb + 1) * 128, :],
                                                    in_=out_f[:],
                                                    inc=sto, inck=16)
                            p.emit()
                ef.close()
            else:
                bcd.close()

            p.wait("sync", sto)
            p.wait("sync", ldx)
            p.emit()

    return nc, dbg


def kernel(**inputs):
    nc, _ = build()
    wmap = {
        "wq": inputs["L0_attn_wq"], "bq": inputs["L0_attn_bq"],
        "wk": inputs["L0_attn_wk"], "bk": inputs["L0_attn_bk"],
        "wv": inputs["L0_attn_wv"], "bv": inputs["L0_attn_bv"],
        "wo": inputs["L0_attn_wo"], "bo": inputs["L0_attn_bo"],
        "gate_w": inputs["L0_ffn_gate_w"], "gu_w": inputs["L0_ffn_gu_w"],
        "dp_w": inputs["L0_ffn_dp_w"],
        "sh_gate_w": inputs["L0_ffn_sh_gate_w"], "sh_up_w": inputs["L0_ffn_sh_up_w"],
        "sh_down_w": inputs["L0_ffn_sh_down_w"], "sh_gate1_w": inputs["L0_ffn_sh_gate1_w"],
    }
    wmap = {k: np.ascontiguousarray(np.asarray(v, dtype=np.float32)) for k, v in wmap.items()}
    xfull = np.asarray(inputs["L0_attn_x"], dtype=np.float32)
    in_maps = [dict(wmap, x=np.ascontiguousarray(xfull[b])) for b in range(8)]
    res = run_bass_kernel_spmd(nc, in_maps, core_ids=list(range(8)))
    return np.stack([r["out"] for r in res.results], axis=0)
